# version 03
# baseline (speedup 1.0000x reference)
import numpy as np
import ml_dtypes
from contextlib import ExitStack

import concourse.bass as bass
import concourse.mybir as mybir
from concourse.bass_utils import run_bass_kernel_spmd

F32 = mybir.dt.float32
BF16 = mybir.dt.bfloat16
F32R = mybir.dt.float32r
AF = mybir.ActivationFunctionType
ALU = mybir.AluOpType
AX = mybir.AxisListType
BF = ml_dtypes.bfloat16

B, T, DM = 4, 4096, 1024
TOWN = 2048
HID, HH = 2048, 1024
NST, DTR = 16, 64
MEGA = 508
NCH = 4
CHT = 127
NMEGA = 8
TAIL = 32
MEGAS = [(m * MEGA, MEGA) for m in range(NMEGA)] + [(NMEGA * MEGA, TAIL)]
RG = [[0, 1], [2, 3], [4, 5], [6, 7]]
ENGS = ("tensor", "vector", "scalar", "gpsimd", "sync")
NEG_BIG = -1e30

_NC_CACHE = {}
import os as _os
DBG = _os.environ.get("KDBG", "") == "1"


class P:

    NPOOL = {"tensor": 5, "vector": 5, "scalar": 4, "gpsimd": 2,
             "sync.d": 3, "gpsimd.d": 3, "scalar.d": 1, "vector.d": 1}

    def __init__(self, nc, stack, nsem=None):
        self.nc = nc
        self.pool = []
        self.engpool = {}
        i = 0
        for e, n in self.NPOOL.items():
            self.engpool[e] = list(range(i, i + n))
            for j in range(n):
                self.pool.append(stack.enter_context(nc.semaphore(f"s_{e}{j}".replace(".", "_"))))
            i += n
        self.barsem = stack.enter_context(nc.semaphore("s_bar"))
        self.barcnt = 0
        self.poolcnt = [0] * len(self.pool)
        self.edges = {}
        self.engrot = {e: 0 for e in self.NPOOL}
        self.ops = {e: [] for e in ENGS}

    def _edge(self, name):
        if name not in self.edges:
            self.edges[name] = [None, 0]
        return self.edges[name]

    def _assign(self, edge, eng, dma):
        key = (eng + ".d") if dma else eng
        if edge[0] is None or edge[0] not in self.engpool[key]:
            pool = self.engpool[key]
            edge[0] = pool[self.engrot[key] % len(pool)]
            self.engrot[key] += 1
        return edge[0]

    def op(self, eng, fn, after=(), then=(), dma=0, fence=False):
        waits = []
        for a in (after if isinstance(after, (list, tuple)) else (after,)):
            if a is None:
                continue
            e = self._edge(a)
            if e[1] > 0 and e[0] is not None:
                waits.append((self.pool[e[0]], e[1]))
        incs = []
        tlist = [t for t in (then if isinstance(then, (list, tuple)) else (then,)) if t is not None]
        if tlist:
            e0 = self._edge(tlist[0])
            si = self._assign(e0, eng, dma)
            amt = 16 if dma else 1
            self.poolcnt[si] += amt
            e0[1] = self.poolcnt[si]
            incs.append((self.pool[si], amt))
            for tname in tlist[1:]:
                e = self._edge(tname)
                e[0], e[1] = e0[0], e0[1]

        def run(engine, fn=fn, waits=waits, incs=incs, fence=fence):
            for s, v in waits:
                engine.wait_ge(s, v)
            last = fn(engine)
            if incs:
                if fence:
                    engine.drain()
                    for s, amt in incs:
                        engine.sem_inc(s, amt)
                else:
                    assert last is not None, "op with then= must return an instruction"
                    for s, amt in incs:
                        last.then_inc(s, amt)
        self.ops[eng].append(run)

    def dma(self, eng, dst, src, after=(), then=(), **kw):
        self.op(eng, lambda e: e.dma_start(dst, src, **kw), after=after, then=then, dma=1)

    def barrier(self, engs=ENGS):
        self.barcnt += len(engs)
        tv = self.barcnt
        sem = self.barsem
        for eng in engs:
            def f(engine, sem=sem, tv=tv):
                engine.drain()
                engine.sem_inc(sem, 1)
                engine.wait_ge(sem, tv)
            self.ops[eng].append(f)

    def finalize(self, block):
        for eng in ENGS:
            getattr(block, eng)(lambda handle, eng=eng: [f(handle) for f in self.ops[eng]])


class Arena:
    def __init__(self, nc, words, name="arena"):
        self.h = nc.alloc_sbuf_tensor(name, [128, words], F32)
        self.off = 0
        self.words = words
        self.marks = []
        self.peak = 0

    def f32(self, words, rows=128):
        assert self.off + words <= self.words, f"arena overflow {self.off}+{words}>{self.words}"
        ap = self.h.ap()[0:rows, self.off:self.off + words]
        self.off += words
        self.peak = max(self.peak, self.off)
        return ap

    def bf16(self, elems, rows=128):
        assert elems % 2 == 0
        return self.f32(elems // 2, rows).bitcast(BF16)

    def mark(self):
        self.marks.append(self.off)

    def reset(self):
        self.off = self.marks.pop()


class Ctx:
    pass


def build_program():
    nc = bass.Bass("TRN2", target_bir_lowering=False)
    stack = ExitStack()
    c = Ctx()
    c.nc = nc

    def inp(name, shape, dt=F32):
        return nc.dram_tensor(name, list(shape), dt, kind="ExternalInput").ap()

    c.d = d = {}
    d["x"] = inp("x", [T, DM])
    d["w_inT"] = inp("w_inT", [DM, HID])
    d["convw"] = inp("convw", [128, 24])
    d["convb"] = inp("convb", [128, 8])
    d["w_xprojT"] = inp("w_xprojT", [HH, 96])
    d["w_dtT_ext"] = inp("w_dtT_ext", [67, HH])
    d["alog_rep"] = inp("alog_rep", [128, NST])
    d["d_cols"] = inp("d_cols", [128, 8])
    d["w_outT"] = inp("w_outT", [HH, DM])
    d["tri"] = inp("tri", [128, 128])
    d["eye"] = inp("eye", [128, 128])
    d["flags"] = inp("flags", [128, 8])
    d["router_wT"] = inp("router_wT", [DM, 8])
    d["router_b"] = inp("router_b", [128, 8])
    d["iota8"] = inp("iota8", [128, 8])
    d["iota8p"] = inp("iota8p", [128, 8])
    d["expert_wT"] = inp("expert_wT", [8, DM, DM])
    d["expert_b"] = inp("expert_b", [8, DM])
    d["wq"] = inp("wq", [DM, DM], BF16)
    d["wk"] = inp("wk", [DM, DM], BF16)
    d["wv"] = inp("wv", [DM, DM], BF16)
    d["wo"] = inp("wo", [DM, DM], BF16)
    d["proj"] = inp("proj", [64, 64], BF16)
    d["la_w1"] = inp("la_w1", [DM, 4096], BF16)
    d["la_w2ext"] = inp("la_w2ext", [4097, DM], BF16)
    d["la_b1c"] = inp("la_b1c", [128, 32])
    d["g1c"] = inp("g1c", [128, 8])
    d["b1c"] = inp("b1c", [128, 8])
    d["g2c"] = inp("g2c", [128, 8])
    d["b2c"] = inp("b2c", [128, 8])
    d["triu_pure"] = inp("triu_pure", [128, 128], BF16)
    for _l in range(2):
        d[f"ffqkv{_l}"] = inp(f"ffqkv{_l}", [DM, 1536], BF16)
        d[f"ffwo{_l}"] = inp(f"ffwo{_l}", [512, DM], BF16)
        d[f"ffw1_{_l}"] = inp(f"ffw1_{_l}", [DM, 5460], BF16)
        d[f"ffw2e{_l}"] = inp(f"ffw2e{_l}", [2731, DM], BF16)
        d[f"ffqs{_l}"] = inp(f"ffqs{_l}", [128, 64])
        d[f"ffks{_l}"] = inp(f"ffks{_l}", [128, 64])
        d[f"fflng{_l}"] = inp(f"fflng{_l}", [128, 8])
        d[f"fflnb{_l}"] = inp(f"fflnb{_l}", [128, 8])
        d[f"ffb1a{_l}"] = inp(f"ffb1a{_l}", [128, 22])
        d[f"ffb1g{_l}"] = inp(f"ffb1g{_l}", [128, 22])
    d["rcq"] = inp("rcq", [TOWN, 64])
    d["rsq"] = inp("rsq", [TOWN, 64])
    d["rck"] = inp("rck", [TOWN, 64])
    d["rsk"] = inp("rsk", [TOWN, 64])
    d["lmask"] = inp("lmask", [5, 128, 640], BF16)
    d["onescol"] = inp("onescol", [128, 8], BF16)
    d["onesrow"] = inp("onesrow", [1, 128], BF16)
    c.out = nc.dram_tensor("out", [TOWN, DM], F32, kind="ExternalOutput").ap()
    c.taps = {}
    if DBG:
        for nm, shape, dt in [("xlnT", [128, 8 * 510], BF16), ("uT", [128, 8 * 510], BF16),
                              ("ucT", [128, 8 * 510], BF16), ("zT", [128, 8 * 508], BF16),
                              ("xdblbf", [67, 512], BF16), ("xdbl", [32, 512], F32),
                              ("dt0", [128, 1024], BF16), ("w0", [128, 16 * 1024], BF16),
                              ("y0", [128, 1024], F32), ("ytT", [128, 8 * 508], BF16),
                              ("gev0", [128, 1024], F32), ("gev1", [128, 1024], F32),
                              ("mb0", [128, 1024], F32), ("mb1", [128, 1024], F32),
                              ("eb0", [128, 1024], BF16), ("eb1", [128, 1024], BF16),
                              ("ya0", [128, 1024], F32), ("gv0", [128, 1024], BF16),
                              ("mbn0", [128, 1024], BF16), ("wpre", [128, 16 * 1024], BF16),
                              ("g6raw", [128, 1024], F32), ("w0attri", [128, 1024], BF16),
                              ("maskw", [128, 128], F32), ("x1t", [128, 1024], F32),
                              ("lnTf", [128, 1024], F32), ("lgt", [128, 8], F32),
                              ("gat", [128, 8 * 508], BF16), ("wop", [128, 8 * 508], BF16),
                              ("l1xz", [128, 1024], F32), ("l1qph", [128, 4096], BF16),
                              ("l1kph", [128, 4096], BF16), ("l1vtk", [128, 1024], BF16),
                              ("l1att", [128, 1024], BF16), ("l1xin1", [128, 1024], F32),
                              ("l1pref", [128, 520], BF16), ("l1h2", [128, 1024], BF16),
                              ("l1gel", [128, 256], BF16), ("l1den", [128, 4], F32)]:
            c.taps[nm] = nc.dram_tensor("tap_" + nm, shape, dt, kind="ExternalOutput").ap()

    c.xres_d = nc.dram_tensor("xres_d", [T, DM], F32, kind="Internal").ap()
    c.cc_xdbl_in = nc.dram_tensor("cc_xdbl_in", [96, MEGA], F32, kind="Internal").ap()
    c.cc_xdbl_out = nc.dram_tensor("cc_xdbl_out", [192, MEGA], F32, kind="Internal").ap()
    c.cc_wo_in = nc.dram_tensor("cc_wo_in", [DM, MEGA], BF16, kind="Internal").ap()
    c.cc_wo_out = nc.dram_tensor("cc_wo_out", [2 * DM, MEGA], BF16, kind="Internal").ap()
    c.xres2_d = nc.dram_tensor("xres2_d", [TOWN, DM], F32, kind="Internal").ap()
    c.cc_kv_in = nc.dram_tensor("cc_kv_in", [128, 520], F32, kind="Internal").ap()
    c.cc_kv_out = nc.dram_tensor("cc_kv_out", [256, 520], F32, kind="Internal").ap()
    c.qph_d = nc.dram_tensor("qph_d", [DM, TOWN], BF16, kind="Internal").ap()
    c.kph_d = nc.dram_tensor("kph_d", [DM, TOWN], BF16, kind="Internal").ap()
    c.vtk_d = nc.dram_tensor("vtk_d", [TOWN, DM], BF16, kind="Internal").ap()
    c.xinn_d = nc.dram_tensor("xinn_d", [TOWN, DM], F32, kind="Internal").ap()
    c.xs_d = nc.dram_tensor("xs_d", [TOWN, DM], F32, kind="Internal").ap()
    c.ga_d = nc.dram_tensor("ga_d", [2730, TOWN], BF16, kind="Internal").ap()
    c.cc_sh_in = nc.dram_tensor("cc_sh_in", [1, DM], F32, kind="Internal").ap()
    c.cc_sh_out = nc.dram_tensor("cc_sh_out", [2, DM], F32, kind="Internal").ap()
    c.cc_kvh_in = nc.dram_tensor("cc_kvh_in", [128, 4096], BF16, kind="Internal").ap()
    c.cc_kvh_out = nc.dram_tensor("cc_kvh_out", [256, 4096], BF16, kind="Internal").ap()

    p = P(nc, stack)
    c.p = p
    A = Arena(nc, 36200)
    c.A = A

    ps_h = stack.enter_context(nc.psum_tensor("ps", [128, 4096], F32))
    c.PS = PS = ps_h.ap()
    c.bank = lambda i, n=1: PS[:, i * 512:(i + n) * 512]
    c.bank_bf = lambda i, n=1: PS.bitcast(BF16)[:, i * 1024:(i + n) * 1024]

    c.tri_sb = A.bf16(128)
    c.eye_sb = A.bf16(128)
    c.eyef_sb = A.f32(128)
    c.arep = A.f32(NST)
    c.anegrep = A.f32(NST)
    c.convw_sb = A.f32(24)
    c.convb_sb = A.f32(8)
    c.dcols_sb = A.f32(8)
    c.flags_sb = A.f32(8)
    c.scr = A.f32(1024)
    c.scr2 = A.f32(1024)

    c.lnc = A.f32(16)
    c.sqd = A.bf16(2 * 1024)

    A.mark()
    c.w_inT_sb = A.bf16(8 * 2048)
    c.w_dtT_sb = A.bf16(1024, rows=67)
    c.w_xprojT_sb = A.bf16(8 * 96)
    c.w_outT_sb = A.bf16(8 * 1024)
    c.w_allr = nc.alloc_sbuf_tensor("w_all_r", [128, NST * 1024], F32R).ap()
    c.trif_sb = A.f32(128)
    c.trifr_sb = nc.alloc_sbuf_tensor("trif_r", [128, 128], F32R).ap()
    c.dtb = [A.bf16(1024) for _ in range(2)]
    c.utokb = [A.bf16(1024) for _ in range(2)]
    c.bcb = [A.f32(32) for _ in range(2)]
    _eb = A.bf16(1024)
    _einvb = A.bf16(1024)
    c.eb = [_eb, _eb]
    c.einvb = [_einvb, _einvb]
    c.gevb = [A.f32(1024) for _ in range(2)]
    c.mbufb = [A.f32(1024) for _ in range(2)]
    c.ysb = [A.f32(1024) for _ in range(2)]
    SLE = MEGA + 2
    c.SLE = SLE
    reg1 = A.f32(8 * SLE // 2)
    c.xlnT = reg1.bitcast(BF16)
    reg2 = A.f32(8 * SLE // 2)
    c.uT = reg2.bitcast(BF16)
    c.ucT = A.bf16(8 * SLE)
    c.zT = A.bf16(8 * MEGA)
    c.xdbl = A.f32(MEGA + 4, rows=32)
    c.xdbl_bf = A.bf16(MEGA + 4, rows=67)
    c.ytT = reg2[:, 0:(8 * MEGA) // 2].bitcast(BF16)
    c.woutp = reg1[:, 0:(8 * MEGA) // 2].bitcast(BF16)
    c.halocol = A.bf16(16)
    mamba_kb = A.off * 4 / 1024

    emit_preload(c)
    p.barrier()
    emit_mamba(c)
    p.barrier()
    A.reset()

    A.mark()
    c.xres = A.f32(16 * 1024)
    emit_load_xres(c)
    p.barrier()
    emit_moe(c)
    p.barrier()
    for j in range(16):
        p.dma("sync", c.xres2_d[j * 128:(j + 1) * 128, :], c.xres[:, j * 1024:(j + 1) * 1024],
              then="x2s")
    p.op("sync", lambda e: None, after="x2s")
    p.barrier()
    A.reset()
    A.reset()
    A.mark()
    emit_l1attn(c)
    p.barrier()
    A.reset()
    A.mark()
    emit_l1ffn(c)
    p.barrier()
    A.reset()

    p.barrier()
    with nc.Block() as block:
        p.finalize(block)
    stack.close()
    return nc


def emit_preload(c):
    p, d = c.p, c.d
    p.dma("gpsimd", c.w_inT_sb.rearrange("p (s n) -> p s n", s=8),
          d["w_inT"].rearrange("(s p) n -> p s n", p=128), then="ld")
    p.dma("gpsimd", c.w_dtT_sb, d["w_dtT_ext"], then="ld")
    p.dma("gpsimd", c.w_xprojT_sb.rearrange("p (s n) -> p s n", s=8),
          d["w_xprojT"].rearrange("(s p) n -> p s n", p=128), then="ld")
    p.dma("gpsimd", c.w_outT_sb.rearrange("p (s n) -> p s n", s=8),
          d["w_outT"].rearrange("(s p) n -> p s n", p=128), then="ld")
    p.dma("gpsimd", c.tri_sb, d["tri"], then="ld")
    p.dma("gpsimd", c.eye_sb, d["eye"], then="ld")
    p.dma("sync", c.eyef_sb, d["eye"], then="ld")
    p.dma("sync", c.trif_sb, d["tri"], then="ld")
    p.dma("sync", c.convw_sb, d["convw"], then="ld")
    p.dma("sync", c.convb_sb, d["convb"], then="ld")
    p.dma("sync", c.dcols_sb, d["d_cols"], then="ld")
    p.dma("sync", c.flags_sb, d["flags"], then="ld")
    p.dma("sync", c.scr[:, 0:NST], d["alog_rep"], then="ld")
    p.dma("sync", c.xres_d, d["x"], then="xrd")

    p.op("scalar", lambda s: s.activation(c.arep, c.scr[:, 0:NST], AF.Exp, scale=1.0),
         after="ld", then="arep0")
    p.op("scalar", lambda s: s.activation(c.trifr_sb, c.trif_sb, AF.Copy),
         after="ld", then="arep0")

    def f_init(v):
        v.tensor_scalar_mul(c.anegrep, c.arep, 1.0)
        v.drain()
        v.tensor_scalar_mul(c.arep, c.arep, -1.0)
        v.memset(c.xdbl_bf[64:67, :], 1.0)
        v.memset(c.xlnT[:, :], 0.0)
        v.memset(c.uT[:, :], 0.0)
        v.memset(c.ucT[:, :], 0.0)
        v.memset(c.scr[:, :], 0.0)
        v.drain()
        for _k in range(NST):
            v.tensor_copy(c.w_allr[:, _k * 1024:(_k + 1) * 1024], c.scr[:, :])
        v.memset(c.dtb[0][:, :], 0.0)
        v.memset(c.dtb[1][:, :], 0.0)
        v.memset(c.ysb[0][:, :], 0.0)
        return v.memset(c.ysb[1][:, :], 0.0)
    p.op("vector", f_init, after=("ld", "arep0"), then="init")


def ln_tile(c, src, rows, after, tag, f32out=None, double_ln=False):
    p, lnc, sqd = c.p, c.lnc, c.sqd

    def f1(v):
        v.tensor_reduce(lnc[0:rows, 0:1], src[0:rows, :], AX.X, ALU.add)
        v.drain()
        return v.tensor_scalar_mul(lnc[0:rows, 0:1], lnc[0:rows, 0:1], 1.0 / DM)
    p.op("vector", f1, after=after, then=tag + "a")

    def f2(s):
        return s.activation(sqd[0:rows, 1024:2048], src[0:rows, :], AF.Square,
                            accum_out=lnc[0:rows, 1:2])
    p.op("scalar", f2, after=after, then=tag + "b")

    def f3(v):
        v.tensor_scalar(lnc[0:rows, 2:3], lnc[0:rows, 1:2], 1.0 / DM, None, ALU.mult)
        v.tensor_tensor(lnc[0:rows, 3:4], lnc[0:rows, 0:1], lnc[0:rows, 0:1], ALU.mult)
        v.drain()
        v.tensor_sub(lnc[0:rows, 2:3], lnc[0:rows, 2:3], lnc[0:rows, 3:4])
        v.drain()
        return v.tensor_scalar_add(lnc[0:rows, 2:3], lnc[0:rows, 2:3], 1e-5)
    p.op("vector", f3, after=(tag + "a", tag + "b"), then=tag + "c")

    p.op("scalar", lambda s: s.activation(lnc[0:rows, 4:5], lnc[0:rows, 2:3], AF.Sqrt),
         after=tag + "c", then=tag + "d")

    def f5(v):
        v.reciprocal(lnc[0:rows, 5:6], lnc[0:rows, 4:5])
        v.drain()
        v.tensor_tensor(lnc[0:rows, 6:7], lnc[0:rows, 0:1], lnc[0:rows, 5:6], ALU.mult)
        v.drain()
        return v.tensor_scalar_mul(lnc[0:rows, 6:7], lnc[0:rows, 6:7], -1.0)
    p.op("vector", f5, after=tag + "d", then=tag + "e")

    dep = tag + "e"
    sc, s2 = 5, 6
    if double_ln:
        def f5b(v):
            v.tensor_tensor(lnc[0:rows, 13:14], lnc[0:rows, 5:6], lnc[0:rows, 5:6], ALU.mult)
            v.drain()
            return v.tensor_scalar(lnc[0:rows, 13:14], lnc[0:rows, 13:14], -1e-5, 1.0 + 1e-5,
                                   ALU.mult, ALU.add)
        p.op("vector", f5b, after=tag + "e", then=tag + "f")
        p.op("scalar", lambda s: s.activation(lnc[0:rows, 14:15], lnc[0:rows, 13:14], AF.Sqrt),
             after=tag + "f", then=tag + "g")

        def f5d(v):
            v.reciprocal(lnc[0:rows, 15:16], lnc[0:rows, 14:15])
            v.drain()
            v.tensor_tensor(lnc[0:rows, 7:8], lnc[0:rows, 5:6], lnc[0:rows, 15:16], ALU.mult)
            return v.tensor_tensor(lnc[0:rows, 12:13], lnc[0:rows, 6:7], lnc[0:rows, 15:16],
                                   ALU.mult)
        p.op("vector", f5d, after=tag + "g", then=tag + "h")
        dep = tag + "h"
        sc, s2 = 7, 12

    if f32out is None:
        def f6(v):
            return v.tensor_scalar(sqd[0:rows, 0:1024], src[0:rows, :],
                                   lnc[0:rows, sc:sc + 1], lnc[0:rows, s2:s2 + 1],
                                   ALU.mult, ALU.add)
    elif double_ln:
        def f6(v):
            v.tensor_scalar(f32out[0:rows, :], src[0:rows, :],
                            lnc[0:rows, 5:6], lnc[0:rows, 6:7], ALU.mult, ALU.add)
            return v.tensor_scalar(sqd[0:rows, 0:1024], src[0:rows, :],
                                   lnc[0:rows, 7:8], lnc[0:rows, 12:13], ALU.mult, ALU.add)
    else:
        def f6(v):
            v.tensor_scalar(f32out[0:rows, :], src[0:rows, :],
                            lnc[0:rows, 5:6], lnc[0:rows, 6:7], ALU.mult, ALU.add)
            v.drain()
            return v.tensor_copy(sqd[0:rows, 0:1024], f32out[0:rows, :])
    aft = (after,) if isinstance(after, str) else tuple(after)
    p.op("vector", f6, after=(dep,) + aft, then=tag)


def emit_mamba(c):
    p, d = c.p, c.d
    bank, bank_bf, PS = c.bank, c.bank_bf, c.PS
    SLE = c.SLE

    def xslab(buf, s, c0, c1, ext=SLE):
        return buf[:, s * ext + c0:s * ext + c1]

    ci = [0]
    for mi, (M0, ML) in enumerate(MEGAS):
        nt = (ML + 127) // 128
        last_mega = mi == len(MEGAS) - 1

        if mi > 0:
            p.dma("gpsimd", c.xlnT.rearrange("p (s n) -> p s n", s=8)[:, :, 0:1]
                  .rearrange("p s n -> p (s n)"), c.halocol[:, 0:8],
                  after="halo", then="halor", allow_slow_non_contiguous=True)
        for j in range(nt + (0 if last_mega else 1)):
            extra = j == nt
            t0 = (M0 + ML) if extra else (M0 + j * 128)
            rows = 1 if extra else min(128, ML - j * 128)
            p.dma("sync", c.scr[0:rows, :], d["x"][t0:t0 + rows, :],
                  after=("lnT", "sqf"), then="xl")
            ln_tile(c, c.scr, rows, after=("xl", "trp"), tag=f"ln")
            def f_tr(t, rows=rows):
                last = None
                for s in range(8):
                    last = t.transpose(bank_bf(6)[0:128, s * 128:s * 128 + rows],
                                       c.sqd[0:rows, s * 128:(s + 1) * 128],
                                       c.eye_sb[0:rows, 0:rows])
                return last
            p.op("tensor", f_tr, after=("ln", "lnT"), then="trp", fence=True)
            if not extra:
                def f_ev(s, j=j, rows=rows):
                    return s.activation(
                        c.xlnT.rearrange("p (s n) -> p s n", s=8)[:, :, 1 + j * 128:1 + j * 128 + rows],
                        bank_bf(6).rearrange("p (s n) -> p s n", s=8)[:, :, 0:rows], AF.Copy)
            else:
                def f_ev(s, j=j, rows=rows, ML=ML):
                    return s.activation(
                        c.xlnT.rearrange("p (s n) -> p s n", s=8)[:, :, ML + 1:ML + 2],
                        bank_bf(6).rearrange("p (s n) -> p s n", s=8)[:, :, 0:1], AF.Copy)
            p.op("scalar", f_ev, after="trp", then=("lnT", "sqf"))
            if not extra:
                p.dma("gpsimd", c.xres_d[t0:t0 + rows, :], c.sqd[0:rows, 0:1024],
                      after=("ln", "xrd"), then=("lnacc", "sqf"),
                      accum_op=ALU.add)
        if last_mega:
            p.op("vector", lambda v, ML=ML: v.memset(
                c.xlnT.rearrange("p (s n) -> p s n", s=8)[:, :, ML + 1:ML + 2], 0.0),
                after="lnT", then="lnT")
        p.barrier()

        for fs in range(16):
            is_u = fs < 8
            c0, c1 = (0, ML + 2) if is_u else (1, ML + 1)
            bk = fs % 4
            def f_mm(t, fs=fs, c0=c0, c1=c1, bk=bk):
                last = None
                for ds in range(8):
                    last = t.matmul(bank(bk)[:, 0:c1 - c0],
                                    c.w_inT_sb.rearrange("p (s n) -> p s n", s=8)[:, ds, fs * 128:(fs + 1) * 128],
                                    xslab(c.xlnT, ds, c0, c1),
                                    start=(ds == 0), stop=(ds == 7))
                return last
            p.op("tensor", f_mm, after=(f"xzev{bk}", "halor"), then=f"xzmm{bk}", fence=True)
            def f_ev(s, fs=fs, c0=c0, c1=c1, bk=bk, is_u=is_u, ML=ML):
                if is_u:
                    dst = xslab(c.uT, fs, c0, c1)
                else:
                    dst = c.zT[:, (fs - 8) * MEGA:(fs - 8) * MEGA + ML]
                return s.activation(dst, bank(bk)[:, 0:c1 - c0], AF.Copy)
            p.op("scalar", f_ev, after=f"xzmm{bk}", then=(f"xzev{bk}", "xz"))
        p.barrier()

        if DBG and mi == 0:
            p.barrier()
            p.dma("sync", c.taps["xlnT"], c.xlnT, then="tap")
            p.dma("sync", c.taps["uT"], c.uT, then="tap")
            p.barrier()
        for s_ in range(8):
            sb = c.scr if s_ % 2 == 0 else c.scr2
            p.op("vector", lambda v, s_=s_, ML=ML, sb=sb: v.tensor_scalar(
                sb[:, 0:ML], xslab(c.uT, s_, 1, 1 + ML),
                c.convw_sb[:, 3 * s_ + 1:3 * s_ + 2], None, ALU.mult), after=f"cvf{s_ % 2}")
            p.op("vector", lambda v, s_=s_, ML=ML, sb=sb: v.scalar_tensor_tensor(
                sb[:, 0:ML], xslab(c.uT, s_, 0, ML),
                c.convw_sb[:, 3 * s_:3 * s_ + 1], sb[:, 0:ML], ALU.mult, ALU.add))
            p.op("vector", lambda v, s_=s_, ML=ML, sb=sb: v.scalar_tensor_tensor(
                sb[:, 0:ML], xslab(c.uT, s_, 2, 2 + ML),
                c.convw_sb[:, 3 * s_ + 2:3 * s_ + 3], sb[:, 0:ML], ALU.mult, ALU.add),
                then=f"cvv{s_}")
            p.op("scalar", lambda sc, s_=s_, ML=ML, sb=sb: sc.activation(
                xslab(c.ucT, s_, 1, 1 + ML), sb[:, 0:ML], AF.Silu,
                bias=c.convb_sb[:, s_:s_ + 1]), after=f"cvv{s_}", then=("conv", f"cvf{s_ % 2}"))
        p.barrier()

        if DBG and mi == 0:
            p.barrier()
            p.dma("sync", c.taps["ucT"], c.ucT, then="tap")
            p.dma("sync", c.taps["zT"], c.zT, then="tap")
            p.barrier()
        def f_xp(t, ML=ML):
            last = None
            for hs in range(8):
                last = t.matmul(bank(4)[0:96, 0:ML],
                                c.w_xprojT_sb.rearrange("p (s n) -> p s n", s=8)[:, hs, :],
                                xslab(c.ucT, hs, 1, 1 + ML), start=(hs == 0), stop=(hs == 7))
            return last
        p.op("tensor", f_xp, then="xpm", fence=True)
        p.op("scalar", lambda s, ML=ML: s.activation(c.scr[0:96, 0:ML], bank(4)[0:96, 0:ML], AF.Copy),
             after="xpm", then="xpe")
        p.dma("sync", c.cc_xdbl_in[:, 0:ML], c.scr[0:96, 0:ML], after="xpe", then="xpd")
        p.op("gpsimd", lambda gp: gp.collective_compute(
            "AllGather", ALU.bypass, ins=[c.cc_xdbl_in], outs=[c.cc_xdbl_out],
            replica_groups=RG), after="xpd", then="ccx")
        p.dma("gpsimd", c.scr[0:96, 0:ML], c.cc_xdbl_out[0:96, 0:ML], after="ccx", then="ccd")
        p.dma("gpsimd", c.scr[0:96, 0:ML], c.cc_xdbl_out[96:192, 0:ML], after="ccd",
              then="ccd2", accum_op=ALU.add)
        def f_xs(v, ML=ML):
            v.tensor_copy(c.xdbl_bf[0:64, 0:ML], c.scr[0:64, 0:ML])
            return v.tensor_copy(c.xdbl[0:32, 0:ML], c.scr[64:96, 0:ML])
        p.op("vector", f_xs, after="ccd2", then="xdbl")
        p.barrier()

        if DBG and mi == 0:
            p.barrier()
            p.dma("sync", c.taps["xdblbf"][:, 0:MEGA + 4], c.xdbl_bf, then="tap")
            p.dma("sync", c.taps["xdbl"][:, 0:MEGA + 4], c.xdbl, then="tap")
            p.barrier()

        nch = NCH if ML == MEGA else 1
        for lc in range(nch):
            cl = CHT if ML == MEGA else TAIL
            cb = ci[0] % 2
            ci[0] += 1
            w0 = lc * CHT
            def f_dt(t, w0=w0, cl=cl):
                t.matmul(bank(0)[0:cl, :], c.xdbl_bf[0:67, w0:w0 + cl],
                         c.w_dtT_sb[:, 0:512], start=True, stop=True)
                return t.matmul(bank(1)[0:cl, :], c.xdbl_bf[0:67, w0:w0 + cl],
                                c.w_dtT_sb[:, 512:1024], start=True, stop=True)
            p.op("tensor", f_dt, after=("xdbl", "gfree1", f"dtf{cb}"), then="dtm", fence=True)
            p.op("scalar", lambda s, cl=cl: s.activation(
                c.scr[0:cl, :], bank(0, 2)[0:cl, 0:1024], AF.Exp),
                after=("dtm", "scrf"), then="sp1")
            p.op("scalar", lambda s, cl=cl, cb=cb: s.activation(
                c.dtb[cb][0:cl, :], c.scr[0:cl, :], AF.Ln, bias=1.0),
                after="sp1", then=("dtt", "scrf"))
            def f_sm(t, cb=cb):
                t.matmul(bank(2), c.tri_sb, c.dtb[cb][:, 0:512], start=True, stop=True)
                return t.matmul(bank(3), c.tri_sb, c.dtb[cb][:, 512:1024], start=True, stop=True)
            p.op("tensor", f_sm, after=("dtt", "sfree"), then="sm", fence=True)
            p.op("tensor", lambda t, w0=w0, cl=cl: t.transpose(
                bank(4)[0:1 + cl, 0:32], c.xdbl[0:32, w0:w0 + 1 + cl],
                c.eyef_sb[0:32, 0:32]), after=f"bcf{cb}", then="bct", fence=True)
            p.op("vector", lambda v, cb=cb, cl=cl: v.tensor_copy(
                c.bcb[cb][0:cl, :], bank(4)[0:cl, 0:32]), after="bct", then=f"bce{cb}")
            def f_ut(t, w0=w0, cl=cl):
                last = None
                for s_ in range(8):
                    last = t.transpose(
                        PS.bitcast(BF16)[0:cl, 5 * 1024 + s_ * 128:5 * 1024 + (s_ + 1) * 128],
                        xslab(c.ucT, s_, 1 + w0, 1 + w0 + cl), c.eye_sb)
                return last
            p.op("tensor", f_ut, after=("bce%d" % cb, f"utf{cb}", "ytf"), then="utt", fence=True)
            p.op("scalar", lambda s, cb=cb, cl=cl: s.activation(
                c.utokb[cb][0:cl, :], PS.bitcast(BF16)[0:cl, 5 * 1024:6 * 1024], AF.Copy),
                after="utt", then=f"ute{cb}")

            for n in range(NST):
                nb = n % 2
                gb = (6, 0) if nb == 0 else (7, 1)
                def f_ex(s, n=n, nb=nb):
                    s.activation(c.eb[nb], bank(2, 2)[:, 0:1024], AF.Exp, scale=c.arep[:, n:n + 1])
                    return s.activation(c.einvb[nb], bank(2, 2)[:, 0:1024], AF.Exp,
                                        scale=c.anegrep[:, n:n + 1])
                p.op("scalar", f_ex, after=("sm", "exf0", "exf1"),
                     then=(("exp%d" % nb,) if n < NST - 2 else ("exp%d" % nb, "sfree")))
                def f_w(v, n=n, nb=nb, cb=cb, cl=cl):
                    return v.scalar_tensor_tensor(
                        c.w_allr[0:cl, n * 1024:(n + 1) * 1024], c.utokb[cb][0:cl, :],
                        c.bcb[cb][0:cl, n:n + 1], c.einvb[nb][0:cl, :], ALU.mult, ALU.mult)
                p.op("vector", f_w, after=(f"exp{nb}", f"ute{cb}", f"bce{cb}", f"trif{n}"),
                     then=f"wst{n}")
                def f_tri(t, n=n, gb=gb):
                    wv = c.w_allr
                    tv = c.trifr_sb
                    t.matmul(bank(gb[0])[:, 0:512], tv, wv[:, n * 1024:n * 1024 + 512],
                             start=True, stop=True)
                    return t.matmul(bank(gb[1])[:, 0:512], tv,
                                    wv[:, n * 1024 + 512:(n + 1) * 1024], start=True, stop=True)
                p.op("tensor", f_tri, after=(f"wst{n}", f"hdm{n}", f"gfree{nb}", "sp1"), then=f"trim{n}", fence=True)

                def f_ge(s, nb=nb, gb=gb):
                    s.activation(c.gevb[nb][:, 0:512], bank(gb[0])[:, 0:512], AF.Copy)
                    return s.activation(c.gevb[nb][:, 512:1024], bank(gb[1])[:, 0:512], AF.Copy)
                p.op("scalar", f_ge, after=(f"trim{n}", f"mb{nb}"),
                     then=(f"gev{nb}", f"gfree{nb}", f"trif{n}"))
                p.op("vector", lambda v, nb=nb: v.tensor_tensor(
                    c.mbufb[nb], c.gevb[nb], c.eb[nb], ALU.mult),
                    after=(f"gev{nb}", f"mf{nb}", f"hv{nb}"), then=(f"mb{nb}", f"exf{nb}"))
                p.op("gpsimd", lambda gp, n=n, nb=nb, cl=cl: gp.dma_start(
                    c.w_allr[127:128, n * 1024:(n + 1) * 1024], c.mbufb[nb][cl - 1:cl, :]),
                    after=f"mb{nb}", then=(f"hdm{n}", f"hv{nb}"), dma=1)
                if n == 0:
                    def f_y(v, nb=nb, cb=cb, cl=cl):
                        return v.tensor_scalar(c.ysb[cb][0:cl, :], c.mbufb[nb][0:cl, :],
                                               c.bcb[cb][0:cl, 16:17], None, ALU.mult)
                else:
                    def f_y(v, n=n, nb=nb, cb=cb, cl=cl):
                        return v.scalar_tensor_tensor(
                            c.ysb[cb][0:cl, :], c.mbufb[nb][0:cl, :],
                            c.bcb[cb][0:cl, 16 + n:17 + n], c.ysb[cb][0:cl, :],
                            ALU.mult, ALU.add)
                p.op("vector", f_y, after=((f"mb{nb}",) if n > 0 else (f"mb{nb}", f"yf{cb}")),
                     then=(f"mf{nb}",) if n < NST - 1 else (f"mf{nb}", f"ydone"))

            if DBG and mi == 0 and lc == 0:
                p.barrier()
                p.dma("sync", c.taps["dt0"], c.dtb[cb], then="tap")
                p.dma("sync", c.taps["y0"], c.ysb[cb], then="tap")
                p.dma("sync", c.taps["gev0"], c.gevb[0], then="tap")
                p.dma("sync", c.taps["gev1"], c.gevb[1], then="tap")
                p.dma("sync", c.taps["mb0"], c.mbufb[0], then="tap")
                p.dma("sync", c.taps["mb1"], c.mbufb[1], then="tap")
                p.dma("sync", c.taps["eb0"], c.eb[0], then="tap")
                p.dma("sync", c.taps["eb1"], c.eb[1], then="tap")
                p.barrier()
            def f_yt(t, cb=cb):
                last = None
                for s_ in range(8):
                    last = t.transpose(PS[:, 2048 + s_ * 128:2048 + (s_ + 1) * 128],
                                       c.ysb[cb][:, s_ * 128:(s_ + 1) * 128], c.eyef_sb)
                return last
            p.op("tensor", f_yt, after=("ydone", f"ytf", "bct"), then="ytt", fence=True)
            def f_yte(s, cl=cl, w0=w0):
                return s.activation(
                    c.ytT.rearrange("p (s n) -> p s n", s=8)[:, :, w0:w0 + cl],
                    PS[:, 2048:3072].rearrange("p (s n) -> p s n", s=8)[:, :, 0:cl], AF.Copy)
            p.op("scalar", f_yte, after="ytt",
                 then=("ytf", f"yf{cb}", f"utf{cb}", f"bcf{cb}", f"dtf{cb}"))
        p.barrier()

        if DBG and mi == 0:
            p.barrier()
            p.dma("sync", c.taps["ytT"], c.ytT, then="tap")
            p.barrier()
        if not last_mega:
            p.dma("gpsimd", c.halocol[:, 0:8],
                  c.xlnT.rearrange("p (s n) -> p s n", s=8)[:, :, ML:ML + 1]
                  .rearrange("p s n -> p (s n)"),
                  then="halo", allow_slow_non_contiguous=True)

        for s_ in range(8):
            p.op("vector", lambda v, s_=s_, ML=ML: v.scalar_tensor_tensor(
                c.scr[:, 0:ML], xslab(c.ucT, s_, 1, 1 + ML), c.dcols_sb[:, s_:s_ + 1],
                c.ytT[:, s_ * MEGA:s_ * MEGA + ML], ALU.mult, ALU.add),
                after=("g3",), then="g1")
            p.op("scalar", lambda sc, s_=s_, ML=ML: sc.activation(
                c.sqd[:, 0:ML], c.zT[:, s_ * MEGA:s_ * MEGA + ML], AF.Silu),
                after=("g3",), then="g2")
            p.op("vector", lambda v, s_=s_, ML=ML: v.tensor_tensor(
                c.ytT[:, s_ * MEGA:s_ * MEGA + ML], c.scr[:, 0:ML], c.sqd[:, 0:ML], ALU.mult),
                after=("g1", "g2"), then="g3")
        p.barrier()

        if DBG and mi == 0:
            p.barrier()
            p.dma("sync", c.taps["gat"], c.ytT, then="tap")
            p.barrier()
        for ds in range(8):
            bk = ds % 4
            def f_wo(t, ds=ds, bk=bk, ML=ML):
                last = None
                for hs in range(8):
                    last = t.matmul(bank(bk)[:, 0:ML],
                                    c.w_outT_sb.rearrange("p (s n) -> p s n", s=8)[:, hs, ds * 128:(ds + 1) * 128],
                                    c.ytT[:, hs * MEGA:hs * MEGA + ML],
                                    start=(hs == 0), stop=(hs == 7))
                return last
            p.op("tensor", f_wo, after=f"woe{bk}", then=f"wom{bk}", fence=True)
            p.op("scalar", lambda s, ds=ds, bk=bk, ML=ML: s.activation(
                c.woutp[:, ds * MEGA:ds * MEGA + ML], bank(bk)[:, 0:ML], AF.Copy),
                after=(f"wom{bk}", "halo"), then=(f"woe{bk}", "wop"))
        if DBG and mi == 0:
            p.barrier()
            p.dma("sync", c.taps["wop"], c.woutp, then="tap")
            p.barrier()
        p.dma("sync", c.cc_wo_in[:, 0:ML].rearrange("(s p) n -> p s n", p=128),
              c.woutp.rearrange("p (s n) -> p s n", s=8)[:, :, 0:ML], after="wop", then="wod")
        p.op("gpsimd", lambda gp: gp.collective_compute(
            "AllGather", ALU.bypass, ins=[c.cc_wo_in], outs=[c.cc_wo_out],
            replica_groups=RG), after="wod", then="cc2")
        p.dma("gpsimd", c.woutp.rearrange("p (s n) -> p s n", s=8)[:, :, 0:ML],
              c.cc_wo_out[0:DM, 0:ML].rearrange("(s p) n -> p s n", p=128),
              after="cc2", then="cc2a")
        p.dma("gpsimd", c.woutp.rearrange("p (s n) -> p s n", s=8)[:, :, 0:ML],
              c.cc_wo_out[DM:2 * DM, 0:ML].rearrange("(s p) n -> p s n", p=128),
              after="cc2a", then="cc2b", accum_op=ALU.add)
        p.barrier()
        for j in range(nt):
            rows = min(128, ML - j * 128)
            def f_ot(t, j=j, rows=rows):
                last = None
                for ds in range(8):
                    last = t.transpose(bank_bf(6)[0:rows, ds * 128:(ds + 1) * 128],
                                       c.woutp[:, ds * MEGA + j * 128:ds * MEGA + j * 128 + rows],
                                       c.eye_sb)
                return last
            p.op("tensor", f_ot, after="osc", then="ott", fence=True)
            p.op("vector", lambda v, rows=rows: v.tensor_copy(
                c.scr[0:rows, 0:DM], bank_bf(6)[0:rows, 0:1024]), after="ott", then="osc")
            p.dma("gpsimd", c.xres_d[M0 + j * 128:M0 + j * 128 + rows, :], c.scr[0:rows, 0:DM],
                  after=("osc", "xrd"), then="oad", accum_op=ALU.add)
            p.op("vector", lambda v: v.engine_nop(), after="oad", then=None)
        p.barrier()


def emit_load_xres(c):
    p = c.p
    for j in range(16):
        p.dma("sync", c.scr[:, 0:DM], c.xres_d[j * 128:(j + 1) * 128, :], after="fcc", then="flo")
        p.dma("sync", c.scr2[:, 0:DM], c.xres_d[TOWN + j * 128:TOWN + (j + 1) * 128, :],
              after="fcc", then="flo")
        def f_cmb(v, j=j):
            v.tensor_scalar(c.scr[:, 0:DM], c.scr[:, 0:DM], c.flags_sb[:, 0:1], None, ALU.mult)
            v.drain()
            return v.scalar_tensor_tensor(c.xres[:, j * 1024:(j + 1) * 1024], c.scr2[:, 0:DM],
                                          c.flags_sb[:, 1:2], c.scr[:, 0:DM], ALU.mult, ALU.add)
        p.op("vector", f_cmb, after="flo", then="fcc")


def emit_moe(c):
    p, d = c.p, c.d
    bank, bank_bf, PS = c.bank, c.bank_bf, c.PS
    A = c.A
    A.mark()
    x1lnT = A.bf16(8 * 2048)
    _expw = A.bf16(8 * 1024)
    expw = [_expw, _expw]
    rwT = A.f32(64)
    lnfull = A.f32(1024)
    lnTf = A.f32(1024)
    rb = A.f32(8)
    iot = A.f32(8)
    iotp = A.f32(8)
    eb_sb = A.bf16(1024, rows=8)
    maskw = A.f32(16 * 8)
    mwT = A.bf16(2048, rows=8)

    p.dma("sync", rwT.rearrange("p (s n) -> p s n", s=8),
          d["router_wT"].rearrange("(s p) n -> p s n", p=128), then="mld")
    p.dma("sync", rb, d["router_b"], then="mld")
    p.dma("sync", iot, d["iota8"], then="mld")
    p.dma("sync", iotp, d["iota8p"], then="mld")
    p.dma("gpsimd", eb_sb, d["expert_b"], then="mld")

    for j in range(16):
        xt = c.xres[:, j * 1024:(j + 1) * 1024]
        ln_tile(c, xt, 128, after=("sqf2", "mld", "mtr", "lnff"), tag="mln", f32out=lnfull)
        def f_tr(t, j=j):
            last = None
            for s_ in range(8):
                last = t.transpose(bank_bf(4)[:, s_ * 128:(s_ + 1) * 128],
                                   c.sqd[0:128, s_ * 128:(s_ + 1) * 128], c.eye_sb)
            return last
        p.op("tensor", f_tr, after=("mln", "mev"), then="mtr", fence=True)
        def f_trf(t, j=j):
            last = None
            for s_ in range(8):
                last = t.transpose(PS[:, 3072 + s_ * 128:3072 + (s_ + 1) * 128],
                                   lnfull[0:128, s_ * 128:(s_ + 1) * 128], c.eyef_sb)
            return last
        p.op("tensor", f_trf, after=("mln", "rtf", "lnfe2"), then="mtrf", fence=True)
        p.op("vector", lambda v: v.tensor_copy(lnTf, PS[:, 3072:4096]),
             after=("mtrf", "rtm"), then=("lnfe2", "lnff"))
        def f_ev(s, j=j):
            return s.activation(
                x1lnT.rearrange("p (s n) -> p s n", s=8)[:, :, j * 128:(j + 1) * 128],
                bank_bf(4).rearrange("p (s n) -> p s n", s=8)[:, :, 0:128], AF.Copy)
        p.op("scalar", f_ev, after="mtr", then=("mev", "sqf2", "lnx"))
        def f_rt(t, j=j):
            last = None
            for ds in range(8):
                last = t.matmul(bank(6)[:, 0:8],
                                lnTf[:, ds * 128:(ds + 1) * 128],
                                rwT.rearrange("p (s n) -> p s n", s=8)[:, ds, :],
                                start=(ds == 0), stop=(ds == 7))
            return last
        p.op("tensor", f_rt, after=("lnfe2", "rtf", "mte"), then="rtm", fence=True)
        def f_sf(v, j=j):
            mw = maskw[:, j * 8:(j + 1) * 8]
            v.tensor_tensor(mw, bank(6)[:, 0:8], rb, ALU.add)
            v.drain()
            v.tensor_reduce(c.lnc[:, 0:1], mw, AX.X, ALU.max)
            v.drain()
            v.tensor_scalar(c.lnc[:, 1:2], c.lnc[:, 0:1], -1.0, None, ALU.mult)
            return v.engine_nop()
        p.op("vector", f_sf, after="rtm", then=("sf1", "rtf"))
        if DBG and j == 0:
            p.barrier()
            p.dma("sync", c.taps["lnTf"], lnTf, then="tap")
            p.dma("sync", c.taps["lgt"], maskw[:, 0:8], then="tap")
            p.barrier()
        def f_sf2(s, j=j):
            mw = maskw[:, j * 8:(j + 1) * 8]
            return s.activation(c.sqd[:, 0:8], mw, AF.Exp, bias=c.lnc[:, 1:2],
                                accum_out=c.lnc[:, 2:3])
        p.op("scalar", f_sf2, after="sf1", then="sf2")
        def f_sf3(v, j=j):
            mw = maskw[:, j * 8:(j + 1) * 8]
            v.reciprocal(c.lnc[:, 3:4], c.lnc[:, 2:3])
            v.tensor_scalar(c.scr[:, 0:8], mw, c.lnc[:, 0:1], None, ALU.is_equal)
            v.drain()
            v.tensor_tensor(c.scr[:, 8:16], c.scr[:, 0:8], iot, ALU.mult)
            v.drain()
            v.tensor_scalar(c.scr[:, 8:16], c.scr[:, 8:16], 1000.0, None, ALU.add)
            v.drain()
            v.tensor_reduce(c.lnc[:, 4:5], c.scr[:, 8:16], AX.X, ALU.min)
            v.drain()
            v.tensor_scalar(c.scr[:, 0:8], iotp, c.lnc[:, 4:5], None, ALU.is_equal)
            v.drain()
            return v.tensor_scalar(mw, c.scr[:, 0:8], c.lnc[:, 3:4], None, ALU.mult)
        p.op("vector", f_sf3, after="sf2", then=("sf3", "rtf"))
        def f_mt(t, j=j):
            return t.transpose(bank(5)[0:8, 0:128], maskw[:, j * 8:(j + 1) * 8], c.eyef_sb)
        p.op("tensor", f_mt, after=("sf3", "mtf"), then="mtm", fence=True)
        def f_mte(v, j=j):
            return v.tensor_copy(mwT[0:8, j * 128:(j + 1) * 128], bank(5)[0:8, 0:128])
        p.op("vector", f_mte, after="mtm", then=("mte", "mtf"))
    p.barrier()
    if DBG:
        p.dma("sync", c.taps["maskw"], maskw, then="tap")
        p.dma("sync", c.taps["x1t"], c.xres[:, 0:1024], then="tap")
        p.barrier()

    for e in range(8):
        pb = e % 2
        p.dma("gpsimd", expw[pb].rearrange("p (s n) -> p s n", s=8),
              d["expert_wT"][e].rearrange("(s p) n -> p s n", p=128),
              after=f"ewf{pb}", then=f"ewl{pb}")
        for j in range(16):
            for og in range(2):
                bk = (j * 2 + og) % 4
                def f_em(t, e=e, j=j, og=og, bk=bk, pb=pb):
                    last = None
                    for ds in range(8):
                        last = t.matmul(bank(bk)[:, 0:512],
                                        x1lnT.rearrange("p (s n) -> p s n", s=8)[:, ds, j * 128:(j + 1) * 128],
                                        expw[pb].rearrange("p (s n) -> p s n", s=8)[:, ds, og * 512:(og + 1) * 512],
                                        start=(ds == 0), stop=(ds == 7 and e != 7))
                    if e == 7:
                        last = t.matmul(bank(bk)[:, 0:512], mwT[0:8, j * 128:(j + 1) * 128],
                                        eb_sb[0:8, og * 512:(og + 1) * 512],
                                        start=False, stop=True)
                    return last
                p.op("tensor", f_em, after=(f"ewl{pb}", f"emf{bk}", "mte"),
                     then=f"emm{bk}", fence=True)
                def f_ex2(v, e=e, j=j, og=og, bk=bk):
                    xt = c.xres[:, j * 1024 + og * 512:j * 1024 + (og + 1) * 512]
                    return v.scalar_tensor_tensor(xt, bank(bk)[:, 0:512],
                                                  maskw[:, j * 8 + e:j * 8 + e + 1], xt,
                                                  ALU.mult, ALU.add)
                p.op("vector", f_ex2, after=f"emm{bk}", then=(f"emf{bk}", "moe"))
        p.op("gpsimd", lambda g: g.engine_nop(), after="moe", then=("ewf0", "ewf1"))
    p.barrier()


def emit_l1attn(c):
    p, d = c.p, c.d
    bank, bank_bf, PS = c.bank, c.bank_bf, c.PS
    A = c.A
    QT = 512

    pref = A.bf16(16 * 520)
    kvin = A.bf16(520)
    w2b = A.bf16(8 * 1024)
    trp_sb = A.bf16(128)
    ones_c = A.bf16(2)
    ones_r = A.bf16(128, rows=1)

    p.dma("gpsimd", trp_sb, d["triu_pure"], then="lld")
    p.dma("gpsimd", ones_c, d["onescol"][:, 0:2], then="lld")
    p.dma("gpsimd", ones_r, d["onesrow"], then="lld")

    A.mark()
    hT = A.bf16(8 * QT)
    qraw = A.bf16(8 * QT)
    qphT = A.bf16(8 * QT)
    vtok = A.bf16(4 * 1024)
    kptok = A.bf16(4 * 1024)
    wbuf = A.bf16(8 * 1024)
    projsb = A.bf16(64)
    g1c = A.f32(8); b1c = A.f32(8)
    kvrun = A.f32(520)
    xf = A.f32(1024)

    p.dma("gpsimd", projsb[0:64, :], d["proj"], then="lld")
    p.dma("gpsimd", projsb[64:128, :], d["proj"], then="lld")
    p.dma("sync", g1c, d["g1c"], then="lld")
    p.dma("sync", b1c, d["b1c"], then="lld")
    p.op("vector", lambda v: v.memset(kvrun, 0.0), then="kvacc")

    for q in range(4):
        for jj in range(4):
            j = q * 4 + jj
            p.dma("sync", c.scr[:, 0:DM], c.xres2_d[j * 128:(j + 1) * 128, :],
                  after="lascr", then="x2l")
            ln_tile(c, c.scr, 128, after=("x2l", "latr", "lld", "xif"), tag="laln",
                    f32out=xf, double_ln=True)
            p.op("vector", lambda v: v.engine_nop(), after="laln", then="lascr")
            p.dma("sync", c.xinn_d[j * 128:(j + 1) * 128, :], xf, after="laln",
                  then=("xid", "xif"))
            if DBG and j == 0:
                p.dma("sync", c.taps["l1xz"], xf, after="laln", then="txz")

            def f_tr(t, jj=jj):
                last = None
                for s_ in range(8):
                    last = t.transpose(bank_bf(4)[:, s_ * 128:(s_ + 1) * 128],
                                       c.sqd[0:128, s_ * 128:(s_ + 1) * 128], c.eye_sb)
                return last
            p.op("tensor", f_tr, after=("laln", "htev", "ktev"), then="latr", fence=True)

            def f_ev(s, jj=jj):
                return s.activation(
                    hT.rearrange("p (s n) -> p s n", s=8)[:, :, jj * 128:(jj + 1) * 128],
                    bank_bf(4).rearrange("p (s n) -> p s n", s=8)[:, :, 0:128], AF.Copy)
            p.op("scalar", f_ev, after=("latr", "hTfree"), then="htev0")

            def f_g(v, jj=jj):
                last = None
                for s_ in range(8):
                    hv = hT.rearrange("p (s n) -> p s n", s=8)[:, s_, jj * 128:(jj + 1) * 128]
                    last = v.tensor_scalar(hv, hv, g1c[:, s_:s_ + 1], b1c[:, s_:s_ + 1],
                                           ALU.mult, ALU.add)
                return last
            p.op("vector", f_g, after="htev0", then="htev")

        for tok, wname in ((0, "wq"), (1, "wk")):
            p.dma("sync", wbuf.rearrange("p (s n) -> p s n", s=8),
                  d[wname].rearrange("(s p) n -> p s n", p=128), after="qkall", then="wl")
            for fs in range(8):
                bk = fs % 2
                def f_mm(t, fs=fs, bk=bk):
                    last = None
                    for ds in range(8):
                        last = t.matmul(
                            bank(bk)[:, 0:QT],
                            wbuf.rearrange("p (s n) -> p s n", s=8)[:, ds, fs * 128:(fs + 1) * 128],
                            hT.rearrange("p (s n) -> p s n", s=8)[:, ds, :],
                            start=(ds == 0), stop=(ds == 7))
                    return last
                p.op("tensor", f_mm, after=("wl", "htev", f"qke{bk}"), then=f"qkm{bk}", fence=True)
                p.op("scalar", lambda s, fs=fs, bk=bk: s.activation(
                    qraw[:, fs * QT:(fs + 1) * QT], bank(bk)[:, 0:QT], AF.Copy),
                    after=f"qkm{bk}", then=(f"qke{bk}", "qkraw"))
            for h in range(16):
                sl, ro = h // 2, (h % 2) * 64
                def f_pj(t, sl=sl, ro=ro):
                    return t.matmul(bank(5)[ro:ro + 64, 0:QT], projsb[ro:ro + 64, :],
                                    qraw.rearrange("p (s n) -> p s n", s=8)[ro:ro + 64, sl, :],
                                    start=True, stop=True)
                p.op("tensor", f_pj, after=("qkraw", "phd"), then="pjm", fence=True)
                def f_ph1(s, ro=ro):
                    return s.activation(c.scr[ro:ro + 64, 0:QT], bank(5)[ro:ro + 64, 0:QT], AF.Relu)
                p.op("scalar", f_ph1, after=("pjm", "lascr"), then="ph1")
                def f_ph2(v, ro=ro):
                    return v.tensor_tensor(c.scr2[ro:ro + 64, 0:QT], bank(5)[ro:ro + 64, 0:QT],
                                           c.scr[ro:ro + 64, 0:QT], ALU.subtract)
                p.op("vector", f_ph2, after="ph1", then="ph2")
                def f_ph3(s, ro=ro):
                    return s.activation(c.scr2[ro:ro + 64, 0:QT], c.scr2[ro:ro + 64, 0:QT], AF.Exp)
                p.op("scalar", f_ph3, after="ph2", then="ph3")
                def f_ph4(v, sl=sl, ro=ro):
                    return v.tensor_tensor(
                        qphT.rearrange("p (s n) -> p s n", s=8)[ro:ro + 64, sl, :],
                        c.scr2[ro:ro + 64, 0:QT], c.scr[ro:ro + 64, 0:QT], ALU.add)
                p.op("vector", f_ph4, after="ph3", then="phd")
            p.op("vector", lambda v: v.engine_nop(), after="phd", then="lascr")
            if tok == 0:
                p.dma("sync", c.qph_d[:, q * QT:(q + 1) * QT].rearrange("(s p) n -> p s n", p=128),
                      qphT.rearrange("p (s n) -> p s n", s=8), after="phd", then=("qpd", "qkall"))
                if DBG and q == 0:
                    p.dma("sync", c.taps["l1qph"], qphT, after="phd", then="tqp")
            else:
                p.dma("sync", c.kph_d[:, q * QT:(q + 1) * QT].rearrange("(s p) n -> p s n", p=128),
                      qphT.rearrange("p (s n) -> p s n", s=8), after="phd", then="kpd")
                if DBG and q == 0:
                    p.dma("sync", c.taps["l1kph"], qphT, after="phd", then="tkp")
                for jj in range(4):
                    def f_kt(t, jj=jj):
                        last = None
                        for s_ in range(8):
                            last = t.transpose(
                                bank_bf(4)[:, s_ * 128:(s_ + 1) * 128],
                                qphT.rearrange("p (s n) -> p s n", s=8)[:, s_, jj * 128:(jj + 1) * 128],
                                c.eye_sb)
                        return last
                    p.op("tensor", f_kt, after=("phd", "ktev"), then="ktm", fence=True)
                    p.op("scalar", lambda s, jj=jj: s.activation(
                        kptok[:, jj * 1024:(jj + 1) * 1024], bank_bf(4)[:, 0:1024], AF.Copy),
                        after="ktm", then=("ktev", "ktok"))
                p.op("vector", lambda v: v.engine_nop(), after=("ktok", "kpd"), then="qkall")
        p.dma("sync", wbuf.rearrange("p (s n) -> p s n", s=8),
              d["wv"].rearrange("(s p) n -> p s n", p=128), after="qkall", then="wl")
        for jj in range(4):
            for og in range(2):
                bk = og
                def f_vm(t, jj=jj, og=og, bk=bk):
                    last = None
                    for ds in range(8):
                        last = t.matmul(bank(bk)[:, 0:512],
                                        hT.rearrange("p (s n) -> p s n", s=8)[:, ds, jj * 128:(jj + 1) * 128],
                                        wbuf.rearrange("p (s n) -> p s n", s=8)[:, ds, og * 512:(og + 1) * 512],
                                        start=(ds == 0), stop=(ds == 7))
                    return last
                p.op("tensor", f_vm, after=("wl", f"vme{bk}"), then=f"vmm{bk}", fence=True)
                p.op("scalar", lambda s, jj=jj, og=og, bk=bk: s.activation(
                    vtok[:, jj * 1024 + og * 512:jj * 1024 + (og + 1) * 512],
                    bank(bk)[:, 0:512], AF.Copy), after=f"vmm{bk}", then=(f"vme{bk}", "vtk"))
        p.dma("sync", c.vtk_d[q * QT:(q + 1) * QT, :].rearrange("(a p) n -> p a n", p=128),
              vtok.rearrange("p (a n) -> p a n", a=4), after="vtk", then=("vtd", "hTfree"))
        if DBG and q == 0:
            p.dma("sync", c.taps["l1vtk"], vtok[:, 0:1024], after="vtk", then="tvt")
        for jj in range(4):
            ch = q * 4 + jj
            p.op("scalar", lambda s, ch=ch: s.activation(
                pref[:, ch * 520:(ch + 1) * 520], kvrun, AF.Copy),
                after="kvacc", then="pe")
            for h in range(16):
                ro = (h % 2) * 64
                pr = h // 2
                bkv = 6 if pr < 4 else 7
                co = (pr % 4) * 65
                def f_kv(t, jj=jj, h=h, ro=ro, bkv=bkv, co=co):
                    t.matmul(bank(bkv)[ro:ro + 64, co:co + 64],
                             kptok[:, jj * 1024 + h * 64:jj * 1024 + (h + 1) * 64],
                             vtok[:, jj * 1024 + h * 64:jj * 1024 + (h + 1) * 64],
                             start=True, stop=True, skip_group_check=True)
                    return t.matmul(bank(bkv)[ro:ro + 64, co + 64:co + 65],
                                    kptok[:, jj * 1024 + h * 64:jj * 1024 + (h + 1) * 64],
                                    ones_c[:, 0:1],
                                    start=True, stop=True, skip_group_check=True)
                p.op("tensor", f_kv, after=("pe", "ktok", "vtk") if h == 0 else (),
                     then="kvm" if h == 15 else (), fence=(h == 15))
            def f_kvadd(v):
                v.tensor_tensor(kvrun[:, 0:260], kvrun[:, 0:260], bank(6)[:, 0:260], ALU.add)
                return v.tensor_tensor(kvrun[:, 260:520], kvrun[:, 260:520],
                                       bank(7)[:, 0:260], ALU.add)
            p.op("vector", f_kvadd, after="kvm", then="kvacc")
        p.barrier()

    if DBG:
        p.dma("sync", c.taps["l1pref"], pref[:, 520:1040], then="tpf")
    p.dma("sync", c.cc_kv_in, kvrun, after="kvacc", then="fed")
    p.op("gpsimd", lambda gp: gp.collective_compute(
        "AllGather", ALU.bypass, ins=[c.cc_kv_in], outs=[c.cc_kv_out],
        replica_groups=RG), after="fed", then="cckv")
    p.dma("sync", c.scr[:, 0:520], c.cc_kv_out[0:128, :], after="cckv", then="cckl")
    p.op("vector", lambda v: v.tensor_scalar(kvin, c.scr[:, 0:520], c.flags_sb[:, 1:2],
                                             None, ALU.mult), after="cckl", then="kvld")
    p.dma("sync", w2b.rearrange("p (s n) -> p s n", s=8),
          d["wo"].rearrange("(s p) n -> p s n", p=128), then="wol")
    p.barrier()
    A.reset()

    A.mark()
    qpc = [A.bf16(8 * 128) for _ in range(2)]
    kfc = [A.bf16(8 * 128) for _ in range(2)]
    vce = [A.bf16(16 * 66) for _ in range(2)]
    pmb = [A.bf16(128) for _ in range(2)]
    attc = [A.bf16(1024) for _ in range(2)]
    attT = [A.bf16(1024) for _ in range(2)]
    for b_ in range(2):
        p.op("vector", lambda v, b_=b_: v.memset(
            vce[b_].rearrange("p (h n) -> p h n", h=16)[:, :, 64:65], 1.0), then="vcm")
    for ch in range(16):
        cb = ch % 2
        p.dma("sync", qpc[cb].rearrange("p (s n) -> p s n", s=8),
              c.qph_d[:, ch * 128:(ch + 1) * 128].rearrange("(s p) n -> p s n", p=128),
              after=f"qpcf{cb}", then=f"qpcl{cb}")
        p.dma("sync", kfc[cb].rearrange("p (s n) -> p s n", s=8),
              c.kph_d[:, ch * 128:(ch + 1) * 128].rearrange("(s p) n -> p s n", p=128),
              after=f"qpcf{cb}", then=f"qpcl{cb}")
        p.dma("sync", vce[cb].rearrange("p (h n) -> p h n", h=16)[:, :, 0:64],
              c.vtk_d[ch * 128:(ch + 1) * 128, :].rearrange("p (h n) -> p h n", h=16),
              after=(f"qpcf{cb}", "vcm"), then=f"qpcl{cb}")
        for h in range(16):
            hb = h % 2
            sl, ro = h // 2, (h % 2) * 64
            pr = h // 2
            pco = ch * 520 + (260 if pr >= 4 else 0) + (pr % 4) * 65
            kco = (260 if pr >= 4 else 0) + (pr % 4) * 65
            preg = hb * 128
            oreg = hb * 130
            def f_pp(t, cb=cb, sl=sl, ro=ro, preg=preg):
                return t.matmul(bank(2)[:, preg:preg + 128],
                                kfc[cb].rearrange("p (s n) -> p s n", s=8)[ro:ro + 64, sl, :],
                                qpc[cb].rearrange("p (s n) -> p s n", s=8)[ro:ro + 64, sl, :],
                                start=True, stop=True)
            p.op("tensor", f_pp, after=(f"qpcl{cb}", f"pmf{hb}"), then=f"ppm{hb}", fence=True)

            def f_msk(v, hb=hb, preg=preg):
                return v.tensor_tensor(pmb[hb], bank(2)[:, preg:preg + 128], trp_sb, ALU.mult)
            p.op("vector", f_msk, after=f"ppm{hb}", then=f"pm{hb}")

            def f_att(t, cb=cb, h=h, hb=hb, sl=sl, ro=ro, pco=pco, kco=kco, oreg=oreg):
                t.matmul(bank(3)[:, oreg:oreg + 65], pmb[hb],
                         vce[cb].rearrange("p (h n) -> p h n", h=16)[:, h, 0:65],
                         start=True, stop=False, skip_group_check=True)
                t.matmul(bank(3)[:, oreg:oreg + 65],
                         qpc[cb].rearrange("p (s n) -> p s n", s=8)[ro:ro + 64, sl, :],
                         pref[ro:ro + 64, pco:pco + 65], start=False, stop=False,
                         skip_group_check=True)
                return t.matmul(bank(3)[:, oreg:oreg + 65],
                                qpc[cb].rearrange("p (s n) -> p s n", s=8)[ro:ro + 64, sl, :],
                                kvin[ro:ro + 64, kco:kco + 65], start=False, stop=True,
                                skip_group_check=True)
            p.op("tensor", f_att, after=(f"pm{hb}", "kvld", f"oev{hb}"), then=f"attm{hb}",
                 fence=True)

            def f_den(v, hb=hb, oreg=oreg):
                v.tensor_scalar(c.lnc[:, 8 + hb:9 + hb], bank(3)[:, oreg + 64:oreg + 65],
                                1e-6, None, ALU.add)
                v.drain()
                return v.reciprocal(c.lnc[:, 10 + hb:11 + hb], c.lnc[:, 8 + hb:9 + hb])
            p.op("vector", f_den, after=f"attm{hb}", then=f"den{hb}")

            def f_oev(s, cb=cb, h=h, hb=hb, oreg=oreg):
                return s.activation(attc[cb][:, h * 64:(h + 1) * 64], bank(3)[:, oreg:oreg + 64],
                                    AF.Copy, scale=c.lnc[:, 10 + hb:11 + hb])
            p.op("scalar", f_oev, after=f"den{hb}", then=(f"oev{hb}", f"pmf{hb}", "attdone"))
        if DBG and ch == 0:
            p.dma("sync", c.taps["l1att"], attc[cb], after="attdone", then="tat")
            p.dma("sync", c.taps["l1den"], c.lnc[:, 8:12], after="attdone", then="tdn")
        def f_at(t, cb=cb):
            last = None
            for s_ in range(8):
                last = t.transpose(bank_bf(4)[:, s_ * 128:(s_ + 1) * 128],
                                   attc[cb][:, s_ * 128:(s_ + 1) * 128], c.eye_sb)
            return last
        p.op("tensor", f_at, after=("attdone", "atev"), then="atm", fence=True)
        p.op("scalar", lambda s, cb=cb: s.activation(attT[cb], bank_bf(4)[:, 0:1024], AF.Copy),
             after="atm", then="atev")
        for og in range(2):
            def f_wo(t, cb=cb, og=og):
                last = None
                for isl in range(8):
                    last = t.matmul(
                        bank(og)[:, 0:512],
                        attT[cb][:, isl * 128:(isl + 1) * 128],
                        w2b.rearrange("p (s n) -> p s n", s=8)[:, isl, og * 512:(og + 1) * 512],
                        start=(isl == 0), stop=(isl == 7))
                return last
            p.op("tensor", f_wo, after=("atev", "wol", f"woce{og}"), then=f"wocm{og}", fence=True)
            def f_wev(v, og=og):
                return v.tensor_copy(c.scr[:, og * 512:(og + 1) * 512], bank(og)[:, 0:512])
            p.op("vector", f_wev, after=(f"wocm{og}", "wacc"), then=(f"woce{og}", f"wce{og}"))
        p.dma("gpsimd", c.xinn_d[ch * 128:(ch + 1) * 128, :], c.scr[:, 0:1024],
              after=("wce0", "wce1"), then=("wacc", f"qpcf{cb}"), accum_op=ALU.add)
    p.barrier()
    A.reset()
    if DBG:
        p.dma("sync", c.taps["l1xin1"], c.xinn_d[0:128, :], then="txi")
        p.barrier()

    A.mark()
    h2T = A.bf16(8 * 2048)
    g2c = A.f32(8); b2c = A.f32(8)
    w1b = [A.bf16(8 * 128) for _ in range(2)]
    w2fs = [A.bf16(1024) for _ in range(2)]
    w2brow = A.bf16(1024, rows=1)
    lab1 = A.f32(32)
    gel = A.bf16(256)
    xinn2 = A.f32(2 * 1024)
    x2t = A.f32(1024)
    p.dma("sync", g2c, d["g2c"], then="l2d")
    p.dma("sync", b2c, d["b2c"], then="l2d")
    p.dma("sync", lab1, d["la_b1c"], then="l2d")
    p.dma("sync", w2brow, d["la_w2ext"][4096:4097, :], then="l2d")
    for j in range(16):
        p.dma("sync", xinn2[:, 0:1024], c.xinn_d[j * 128:(j + 1) * 128, :],
              after="ffts", then="xil2")
        ln_tile(c, xinn2[:, 0:1024], 128, after=("xil2", "fftr", "l2d"), tag="ffln")
        p.op("vector", lambda v: v.engine_nop(), after="ffln", then="ffts")

        def f_tr(t, j=j):
            last = None
            for s_ in range(8):
                last = t.transpose(bank_bf(4)[:, s_ * 128:(s_ + 1) * 128],
                                   c.sqd[0:128, s_ * 128:(s_ + 1) * 128], c.eye_sb)
            return last
        p.op("tensor", f_tr, after=("ffln", "ffev"), then="fftr", fence=True)

        def f_ev(s, j=j):
            return s.activation(
                h2T.rearrange("p (s n) -> p s n", s=8)[:, :, j * 128:(j + 1) * 128],
                bank_bf(4).rearrange("p (s n) -> p s n", s=8)[:, :, 0:128], AF.Copy)
        p.op("scalar", f_ev, after="fftr", then="ffev0")

        def f_g(v, j=j):
            last = None
            for s_ in range(8):
                hv = h2T.rearrange("p (s n) -> p s n", s=8)[:, s_, j * 128:(j + 1) * 128]
                last = v.tensor_scalar(hv, hv, g2c[:, s_:s_ + 1], b2c[:, s_:s_ + 1],
                                       ALU.mult, ALU.add)
            return last
        p.op("vector", f_g, after="ffev0", then="ffev")
    if DBG:
        p.dma("sync", c.taps["l1h2"].rearrange("p (s n) -> p s n", s=8),
              h2T.rearrange("p (s n) -> p s n", s=8)[:, :, 0:128], after="ffev", then="th2")
    p.barrier()
    for tb in range(8):
        for fs in range(32):
            fb = fs % 2
            p.dma("sync", w1b[fb].rearrange("p (s n) -> p s n", s=8),
                  d["la_w1"][:, fs * 128:(fs + 1) * 128].rearrange("(s p) n -> p s n", p=128),
                  after=f"w1f{fb}", then=f"w1l{fb}")
            p.dma("sync", w2fs[fb], d["la_w2ext"][fs * 128:(fs + 1) * 128, :],
                  after=f"w1f{fb}", then=f"w1l{fb}")

            def f_w1(t, tb=tb, fs=fs, fb=fb):
                last = None
                for ds in range(8):
                    last = t.matmul(
                        bank(5)[:, 0:256],
                        w1b[fb].rearrange("p (s n) -> p s n", s=8)[:, ds, :],
                        h2T.rearrange("p (s n) -> p s n", s=8)[:, ds, tb * 256:(tb + 1) * 256],
                        start=(ds == 0), stop=(ds == 7))
                return last
            p.op("tensor", f_w1, after=(f"w1l{fb}", "ffev", "gelf"), then="w1m", fence=True)

            def f_gel(s, fs=fs):
                return s.activation(gel[:, 0:256], bank(5)[:, 0:256], AF.Gelu,
                                    bias=lab1[:, fs:fs + 1])
            p.op("scalar", f_gel, after="w1m", then="gel")
            if DBG and tb == 0 and fs == 0:
                p.dma("sync", c.taps["l1gel"], gel, after="gel", then="tgl")
            for tt in range(2):
                for og in range(2):
                    bk = tt * 2 + og
                    def f_w2(t, fs=fs, fb=fb, tt=tt, og=og, bk=bk):
                        last = t.matmul(bank(bk)[:, 0:512], gel[:, tt * 128:(tt + 1) * 128],
                                        w2fs[fb][:, og * 512:(og + 1) * 512],
                                        start=(fs == 0), stop=False, skip_group_check=True)
                        if fs == 31:
                            last = t.matmul(bank(bk)[:, 0:512], ones_r,
                                            w2brow[:, og * 512:(og + 1) * 512],
                                            start=False, stop=True, skip_group_check=True)
                        return last
                    p.op("tensor", f_w2,
                         after=("gel",) + ((f"w2e{bk}",) if fs == 0 else ()),
                         then=("w2m%d" % bk, "w2fin") if fs == 31 else ("gelf", "w2fin"),
                         fence=(fs == 31))
            p.op("vector", lambda v, fb=fb: v.engine_nop(), after="w2fin", then=f"w1f{fb}")
        for tt in range(2):
            j = tb * 2 + tt
            p.dma("sync", xinn2[:, tt * 1024:(tt + 1) * 1024], c.xinn_d[j * 128:(j + 1) * 128, :],
                  after="xi2f", then="xi2l")
            p.dma("sync", x2t, c.xres2_d[j * 128:(j + 1) * 128, :], after="x2tf", then="x2tl")
            for og in range(2):
                bk = tt * 2 + og
                def f_fin(v, tt=tt, og=og, bk=bk):
                    v.tensor_tensor(c.scr2[:, og * 512:(og + 1) * 512], bank(bk)[:, 0:512],
                                    xinn2[:, tt * 1024 + og * 512:tt * 1024 + (og + 1) * 512],
                                    ALU.add)
                    v.drain()
                    return v.tensor_tensor(c.scr2[:, og * 512:(og + 1) * 512],
                                           c.scr2[:, og * 512:(og + 1) * 512],
                                           x2t[:, og * 512:(og + 1) * 512], ALU.add)
                p.op("vector", f_fin, after=("w2m%d" % bk, "xi2l", "x2tl", "s2f"),
                     then=(f"w2e{bk}", "fin%d" % og))
            p.dma("sync", c.xres2_d[j * 128:(j + 1) * 128, :], c.scr2[:, 0:1024],
                  after=("fin0", "fin1"), then=("s2f", "xi2f", "x2tf"))
    p.barrier()
    A.reset()


def emit_l1ffn(c):
    p, d = c.p, c.d
    bank, bank_bf, PS = c.bank, c.bank_bf, c.PS
    A = c.A

    A.mark()
    ones_r = A.bf16(128, rows=1)
    h0q = A.bf16(1024, rows=1)
    h0f = A.f32(1024, rows=1)
    p.dma("gpsimd", ones_r, d["onesrow"], then="fld")

    xfz = A.f32(1024)
    for j in range(16):
        p.dma("sync", c.scr[:, 0:DM], c.xres2_d[j * 128:(j + 1) * 128, :],
              after="flscr", then="flx")
        ln_tile(c, c.scr, 128, after=("flx", "flzd"), tag="flln", f32out=xfz)
        p.op("vector", lambda v: v.engine_nop(), after="flln", then="flscr")
        p.dma("sync", c.xs_d[j * 128:(j + 1) * 128, :], xfz, after="flln", then="flzd")
    p.barrier()

    ph = int(_os.environ.get("KPHASE", "99"))
    for ly in range(2):
        lvl = ph - 4 * ly
        if lvl >= 1:
            emit_l1ffn_layer(c, ly, ones_r, h0q, h0f, lvl)

    A.reset()
    for j in range(16):
        p.dma("sync", c.scr[:, 0:DM], c.xres2_d[j * 128:(j + 1) * 128, :], after="foc3", then="flo3")
        p.dma("sync", c.scr2[:, 0:DM], c.xs_d[j * 128:(j + 1) * 128, :], after="foc3", then="flo3")
        p.op("vector", lambda v: v.tensor_tensor(c.scr2[:, 0:DM], c.scr2[:, 0:DM],
                                                 c.scr[:, 0:DM], ALU.add), after="flo3", then="foc3")
        p.dma("sync", c.out[j * 128:(j + 1) * 128, :], c.scr2[:, 0:DM], after="foc3", then="fo2")
    p.op("sync", lambda e: None, after="fo2")


def emit_shift_halo(c, h0q, h0f, tagp):
    p = c.p
    p.dma("sync", c.cc_sh_in, c.xs_d[TOWN - 1:TOWN, :], after=tagp + "u", then=tagp + "a")
    p.op("gpsimd", lambda gp: gp.collective_compute(
        "AllGather", ALU.bypass, ins=[c.cc_sh_in], outs=[c.cc_sh_out],
        replica_groups=RG), after=tagp + "a", then=tagp + "b")
    p.dma("sync", c.scr[0:1, 0:DM], c.cc_sh_out[0:1, :], after=tagp + "b", then=tagp + "c")

    def f(v):
        v.tensor_scalar(h0f, c.scr[0:1, 0:DM], c.flags_sb[0:1, 1:2], None, ALU.mult)
        v.drain()
        return v.tensor_copy(h0q, h0f)
    p.op("vector", f, after=tagp + "c", then=tagp)


def emit_l1ffn_layer(c, ly, ones_r, h0q, h0f, lvl=99):
    p, d = c.p, c.d
    bank, bank_bf, PS = c.bank, c.bank_bf, c.PS
    A = c.A
    L = f"L{ly}"

    emit_shift_halo(c, h0q, h0f, "sh1" + L)

    A.mark()
    qT_all = A.bf16(4 * 2048)
    kT_all = A.bf16(4 * 2048)
    vtok_all = A.bf16(16 * 512)
    kT_halo = A.bf16(4 * 512)
    v_halo = A.bf16(4 * 512)

    A.mark()
    wqkv_sb = A.bf16(8 * 1536)
    qs_rep = A.f32(64)
    ks_rep = A.f32(64)
    tqc = A.f32(16 * 64); tqs = A.f32(16 * 64)
    tkc = A.f32(16 * 64); tks = A.f32(16 * 64)
    hbf = A.bf16(1024)
    hTt = A.bf16(8 * 128)
    sqv = A.f32(512)
    nrm = A.f32(32)
    qn = A.f32(512)
    rsc1 = A.f32(512)
    rsc2 = A.f32(512)
    qrot = A.bf16(512)

    p.dma("gpsimd", wqkv_sb.rearrange("p (s n) -> p s n", s=8),
          d[f"ffqkv{ly}"].rearrange("(s p) n -> p s n", p=128), then="qld" + L)
    p.dma("sync", qs_rep, d[f"ffqs{ly}"], then="qld" + L)
    p.dma("sync", ks_rep, d[f"ffks{ly}"], then="qld" + L)
    for tb_, nm_ in ((tqc, "rcq"), (tqs, "rsq"), (tkc, "rck"), (tks, "rsk")):
        p.dma("sync", tb_.rearrange("p (a b) -> p a b", a=16),
              d[nm_].rearrange("(a p) b -> p a b", p=128), then="qld" + L)

    def r3(ap, n=8):
        return ap.rearrange("p (a b) -> p a b", a=n)

    for j in range(16):
        p.dma("gpsimd", hbf[:, 0:512], c.xs_d[j * 128:(j + 1) * 128, 0:512],
              after="hfree" + L, then="hld" + L)
        if j == 0:
            p.dma("gpsimd", hbf[1:128, 512:1024], c.xs_d[0:127, 512:1024],
                  after="hfree" + L, then="hld" + L)
            p.op("vector", lambda v: v.tensor_copy(hbf[0:1, 512:1024], h0q[0:1, 512:1024]),
                 after=("hld" + L, "sh1" + L), then="hld" + L)
        else:
            p.dma("gpsimd", hbf[:, 512:1024], c.xs_d[j * 128 - 1:(j + 1) * 128 - 1, 512:1024],
                  after="hfree" + L, then="hld" + L)

        def f_tr(t, j=j):
            last = None
            for s_ in range(8):
                last = t.transpose(bank_bf(4)[:, s_ * 128:(s_ + 1) * 128],
                                   hbf[:, s_ * 128:(s_ + 1) * 128], c.eye_sb)
            return last
        p.op("tensor", f_tr, after=("hld" + L, "qld" + L, "htev" + L, "b4f" + L),
             then="htr" + L, fence=True)
        p.op("scalar", lambda s: s.activation(hTt, bank_bf(4)[:, 0:1024], AF.Copy),
             after="htr" + L, then=("htev" + L, "hfree" + L, "b4f" + L))

        def f_qkv(t, j=j):
            last = None
            for ck in range(3):
                for ds in range(8):
                    last = t.matmul(bank(ck)[:, 0:512],
                                    r3(hTt)[:, ds, :],
                                    r3(wqkv_sb, 8)[:, ds, ck * 512:(ck + 1) * 512],
                                    start=(ds == 0), stop=(ds == 7))
            return last
        p.op("tensor", f_qkv, after=("htev" + L, "qkve" + L), then="qkvm" + L, fence=True)
        p.op("scalar", lambda s, j=j: s.activation(vtok_all[:, j * 512:(j + 1) * 512],
                                                   bank(2)[:, 0:512], AF.Copy),
             after="qkvm" + L, then=("vev" + L, "qkve" + L))

        for pth, bk, srep, tc, ts_, dstT in ((0, 0, qs_rep, tqc, tqs, qT_all),
                                             (1, 1, ks_rep, tkc, tks, kT_all)):
            PL = f"{L}p{pth}"
            no = pth * 16
            p.op("scalar", lambda s, bk=bk: s.activation(sqv, bank(bk)[:, 0:512], AF.Square),
                 after=("qkvm" + L, "nrd" + PL, "sqf" + L), then="sq" + PL)

            def f_n1(v, bk=bk, no=no):
                v.tensor_reduce(r3(nrm[:, no:no + 8], 8)[:, :, 0:1], r3(sqv, 8), AX.X, ALU.add)
                return v.engine_nop()
            p.op("vector", f_n1, after="sq" + PL, then=("n1" + PL, "sqf" + L), fence=True)
            p.op("scalar", lambda s, no=no: s.activation(nrm[:, no + 8:no + 16],
                                                         nrm[:, no:no + 8], AF.Sqrt),
                 after="n1" + PL, then="n2" + PL)

            def f_n3(v, no=no):
                v.tensor_scalar(nrm[:, no + 8:no + 16], nrm[:, no + 8:no + 16], 1e-12, None,
                                ALU.max)
                v.drain()
                return v.reciprocal(nrm[:, no:no + 8], nrm[:, no + 8:no + 16])
            p.op("vector", f_n3, after="n2" + PL, then="n3" + PL)

            def f_ev(s, bk=bk, no=no):
                last = None
                for h in range(8):
                    last = s.activation(qn[:, h * 64:(h + 1) * 64],
                                        bank(bk)[:, h * 64:(h + 1) * 64], AF.Copy,
                                        scale=nrm[:, no + h:no + h + 1])
                return last
            p.op("scalar", f_ev, after=("n3" + PL, "rotd" + PL, "qnf" + L), then="nev" + PL,
                 fence=True)

            def f_rot(v, j=j, srep=srep, tc=tc, ts_=ts_):
                v.tensor_tensor(r3(rsc1), r3(qn), r3(srep, 1).broadcast_to([128, 8, 64]),
                                ALU.mult)
                v.drain()
                v.tensor_scalar(r3(rsc2)[:, :, 0:32], r3(rsc1)[:, :, 32:64], -1.0, None,
                                ALU.mult)
                v.tensor_copy(r3(rsc2)[:, :, 32:64], r3(rsc1)[:, :, 0:32])
                v.drain()
                v.tensor_tensor(r3(rsc1), r3(rsc1),
                                r3(tc, 16)[:, j:j + 1, :].broadcast_to([128, 8, 64]), ALU.mult)
                v.tensor_tensor(r3(rsc2), r3(rsc2),
                                r3(ts_, 16)[:, j:j + 1, :].broadcast_to([128, 8, 64]), ALU.mult)
                v.drain()
                return v.tensor_tensor(qrot, rsc1, rsc2, ALU.add)
            p.op("vector", f_rot, after=("nev" + PL, "qtre" + PL, "qrf" + L),
                 then=("rot" + PL, "qnf" + L))

            def f_qt(t, j=j):
                last = None
                for b_ in range(4):
                    last = t.transpose(bank_bf(4)[:, b_ * 128:(b_ + 1) * 128],
                                       qrot[:, b_ * 128:(b_ + 1) * 128], c.eye_sb)
                return last
            p.op("tensor", f_qt, after=("rot" + PL, "qtev" + PL, "b4f" + L), then="qtr" + PL,
                 fence=True)

            def f_qte(s, j=j, dstT=dstT):
                last = None
                for b_ in range(4):
                    last = s.activation(
                        dstT.rearrange("p (hb n) -> p hb n", hb=4)[:, b_, j * 128:(j + 1) * 128],
                        bank_bf(4)[:, b_ * 128:(b_ + 1) * 128], AF.Copy)
                return last
            p.op("scalar", f_qte, after="qtr" + PL,
                 then=("qtev" + PL, "rotd" + PL, "qtre" + PL, "nrd" + PL,
                       "qrf" + L, "b4f" + L, "qkdone" + L))
    p.barrier()

    p.dma("sync", c.cc_kvh_in[:, 0:2048].rearrange("p (hb n) -> p hb n", hb=4),
          kT_all.rearrange("p (hb n) -> p hb n", hb=4)[:, :, 1536:2048], then="kvh1" + L)
    p.dma("sync", c.cc_kvh_in[:, 2048:4096], vtok_all[:, 12 * 512:16 * 512], then="kvh1" + L)
    p.op("gpsimd", lambda gp: gp.collective_compute(
        "AllGather", ALU.bypass, ins=[c.cc_kvh_in], outs=[c.cc_kvh_out],
        replica_groups=RG), after="kvh1" + L, then="kvh2" + L)
    p.dma("sync", kT_halo, c.cc_kvh_out[0:128, 0:2048], after="kvh2" + L, then="kvh3" + L)
    p.dma("sync", v_halo, c.cc_kvh_out[0:128, 2048:4096], after="kvh2" + L, then="kvh3" + L)

    def f_hf(v):
        v.tensor_scalar(kT_halo, kT_halo, c.flags_sb[:, 1:2], None, ALU.mult)
        return v.tensor_scalar(v_halo, v_halo, c.flags_sb[:, 1:2], None, ALU.mult)
    p.op("vector", f_hf, after="kvh3" + L, then="kvhd" + L)
    p.barrier()
    A.reset()
    if lvl < 2:
        A.reset()
        return

    A.mark()
    lmask = A.bf16(5 * 640)
    wo_sb = A.bf16(4 * 1024)
    smf = A.f32(640)
    expb = A.bf16(640)
    attnT = A.bf16(640)
    otile = A.bf16(512)
    rc8 = A.f32(8)
    p.dma("gpsimd", lmask.rearrange("p (m n) -> p m n", m=5),
          d["lmask"].rearrange("m p n -> p m n"), then="mld" + L)
    p.dma("sync", wo_sb.rearrange("p (s n) -> p s n", s=4),
          d[f"ffwo{ly}"].rearrange("(s p) n -> p s n", p=128), then="mld" + L)

    def kT4(ap):
        return ap.rearrange("p (hb n) -> p hb n", hb=4)

    for blk in range(4):
        for tq in range(4):
            j = blk * 4 + tq
            prevw = 512 - tq * 128
            midx = tq if blk == 0 else 4
            for h in range(8):
                hb, ro = h // 2, (h % 2) * 64

                def f_sim(t, blk=blk, tq=tq, j=j, hb=hb, ro=ro, prevw=prevw):
                    lhs = kT4(qT_all)[ro:ro + 64, hb, j * 128:(j + 1) * 128]
                    if blk == 0:
                        kprev = kT4(kT_halo)[ro:ro + 64, hb, tq * 128:512]
                    else:
                        kprev = kT4(kT_all)[ro:ro + 64, hb,
                                            (blk - 1) * 512 + tq * 128:blk * 512]
                    last = t.matmul(PS[:, 0:prevw], lhs, kprev, start=True, stop=True)
                    if tq > 0:
                        last = t.matmul(PS[:, prevw:512], lhs,
                                        kT4(kT_all)[ro:ro + 64, hb,
                                                    blk * 512:blk * 512 + tq * 128],
                                        start=True, stop=True)
                    last = t.matmul(PS[:, 512:640], lhs,
                                    kT4(kT_all)[ro:ro + 64, hb,
                                                blk * 512 + tq * 128:blk * 512 + tq * 128 + 128],
                                    start=True, stop=True)
                    return last
                p.op("tensor", f_sim, after=("qkdone" + L, "kvhd" + L, "mld" + L, "smv" + L),
                     then="simm" + L, fence=True)

                def f_msk(v, midx=midx):
                    v.tensor_tensor(smf, PS[:, 0:640],
                                    lmask.rearrange("p (m n) -> p m n", m=5)[:, midx, :],
                                    ALU.add)
                    v.drain()
                    v.tensor_reduce(c.lnc[:, 8:9], smf, AX.X, ALU.max)
                    v.drain()
                    return v.tensor_scalar_mul(c.lnc[:, 9:10], c.lnc[:, 8:9], -1.0)
                p.op("vector", f_msk, after=("simm" + L, "exd" + L), then=("smv" + L, "msk" + L))

                def f_exp(s):
                    return s.activation(expb, smf, AF.Exp, bias=c.lnc[:, 9:10],
                                        accum_out=c.lnc[:, 10:11])
                p.op("scalar", f_exp, after=("msk" + L, "atc" + L), then="exp" + L)

                def f_rc(v, h=h):
                    return v.reciprocal(rc8[:, h:h + 1], c.lnc[:, 10:11])
                p.op("vector", f_rc, after="exp" + L, then=("exd" + L, "rcd" + L))

                def f_et(t):
                    last = None
                    for b_ in range(5):
                        last = t.transpose(bank_bf(2)[:, b_ * 128:(b_ + 1) * 128],
                                           expb[:, b_ * 128:(b_ + 1) * 128], c.eye_sb)
                    return last
                p.op("tensor", f_et, after=("exp" + L, "ate" + L), then="etr" + L, fence=True)
                p.op("scalar", lambda s: s.activation(attnT, bank_bf(2)[:, 0:640], AF.Copy),
                     after="etr" + L, then=("ate" + L, "atc" + L))

                def f_av(t, blk=blk, tq=tq, h=h):
                    last = None
                    for kb in range(5):
                        q_ = tq + kb
                        if q_ < 4:
                            if blk == 0:
                                rhs = v_halo[:, q_ * 512 + h * 64:q_ * 512 + (h + 1) * 64]
                            else:
                                tv = (blk - 1) * 4 + q_
                                rhs = vtok_all[:, tv * 512 + h * 64:tv * 512 + (h + 1) * 64]
                        else:
                            tv = blk * 4 + (q_ - 4)
                            rhs = vtok_all[:, tv * 512 + h * 64:tv * 512 + (h + 1) * 64]
                        last = t.matmul(bank(3)[:, h * 64:(h + 1) * 64],
                                        attnT[:, kb * 128:(kb + 1) * 128], rhs,
                                        start=(kb == 0), stop=(kb == 4),
                                        skip_group_check=True)
                    return last
                p.op("tensor", f_av, after=("ate" + L, "oevd" + L), then="avm" + L, fence=True)

            def f_oev(s):
                last = None
                for h in range(8):
                    last = s.activation(otile[:, h * 64:(h + 1) * 64],
                                        bank(3)[:, h * 64:(h + 1) * 64],
                                        AF.Copy, scale=rc8[:, h:h + 1])
                return last
            p.op("scalar", f_oev, after=("avm" + L, "rcd" + L, "otr" + L),
                 then=("oev" + L, "oevd" + L))

            def f_ot(t):
                last = None
                for b_ in range(4):
                    last = t.transpose(bank_bf(4)[:, b_ * 128:(b_ + 1) * 128],
                                       otile[:, b_ * 128:(b_ + 1) * 128], c.eye_sb)
                return last
            p.op("tensor", f_ot, after=("oev" + L, "otev" + L), then="otm" + L, fence=True)
            p.op("scalar", lambda s: s.activation(attnT[:, 0:512], bank_bf(4)[:, 0:512], AF.Copy),
                 after="otm" + L, then=("otev" + L, "otr" + L, "otb" + L))

            for og in range(2):
                def f_wo(t, og=og):
                    last = None
                    for isl in range(4):
                        last = t.matmul(bank(5 + og)[:, 0:512],
                                        attnT[:, isl * 128:(isl + 1) * 128],
                                        wo_sb.rearrange("p (s n) -> p s n", s=4)[:, isl,
                                                                                 og * 512:(og + 1) * 512],
                                        start=(isl == 0), stop=(isl == 3))
                    return last
                p.op("tensor", f_wo, after=("otb" + L, f"woe{og}" + L), then=f"wom{og}" + L,
                     fence=True)
            p.dma("sync", c.scr2[:, 0:DM], c.xs_d[j * 128:(j + 1) * 128, :],
                  after="s1d" + L, then="s1l" + L)

            def f_s1(v):
                v.tensor_tensor(c.scr2[:, 0:512], c.scr2[:, 0:512], bank(5)[:, 0:512], ALU.add)
                return v.tensor_tensor(c.scr2[:, 512:1024], c.scr2[:, 512:1024],
                                       bank(6)[:, 0:512], ALU.add)
            p.op("vector", f_s1, after=("wom0" + L, "wom1" + L, "s1l" + L, "s1w" + L),
                 then=("s1v" + L, "woe0" + L, "woe1" + L))
            p.dma("sync", c.xs_d[j * 128:(j + 1) * 128, :], c.scr2[:, 0:DM],
                  after="s1v" + L, then=("s1w" + L, "s1d" + L))
    p.barrier()
    A.reset()
    A.reset()
    if lvl < 3:
        return

    emit_shift_halo(c, h0q, h0f, "sh2" + L)

    A.mark()
    h2T = A.bf16(8 * 2048)
    lngc = A.f32(8); lnbc = A.f32(8)
    b1ac = A.f32(22); b1gc = A.f32(22)
    w1a = [A.bf16(8 * 128) for _ in range(2)]
    w1g = [A.bf16(8 * 128) for _ in range(2)]
    gg = A.bf16(512)
    gab = A.bf16(512)
    gat = [A.bf16(512) for _ in range(2)]
    w2r = [A.bf16(1024) for _ in range(2)]
    w2brow = A.bf16(1024, rows=1)
    p.dma("sync", lngc, d[f"fflng{ly}"], then="f2l" + L)
    p.dma("sync", lnbc, d[f"fflnb{ly}"], then="f2l" + L)
    p.dma("sync", b1ac, d[f"ffb1a{ly}"], then="f2l" + L)
    p.dma("sync", b1gc, d[f"ffb1g{ly}"], then="f2l" + L)
    p.dma("sync", w2brow, d[f"ffw2e{ly}"][2730:2731, :], then="f2l" + L)

    for j in range(16):
        p.dma("sync", c.scr[:, 0:512], c.xs_d[j * 128:(j + 1) * 128, 0:512],
              after=("fsf" + L,) + (("sh2" + L,) if j == 0 else ()), then="fsl" + L)
        if j == 0:
            p.dma("sync", c.scr[1:128, 512:1024], c.xs_d[0:127, 512:1024],
                  after=("fsf" + L, "sh2" + L), then="fsl" + L)
            p.op("vector", lambda v: v.tensor_copy(c.scr[0:1, 512:1024], h0f[0:1, 512:1024]),
                 after=("fsl" + L, "sh2" + L), then="fsl" + L)
        else:
            p.dma("sync", c.scr[:, 512:1024],
                  c.xs_d[j * 128 - 1:(j + 1) * 128 - 1, 512:1024],
                  after="fsf" + L, then="fsl" + L)
        ln_tile(c, c.scr, 128, after=("fsl" + L, "ftr" + L, "f2l" + L), tag="f2ln" + L)
        p.op("vector", lambda v: v.engine_nop(), after="f2ln" + L, then="fsf" + L)

        def f_tr(t, j=j):
            last = None
            for s_ in range(8):
                last = t.transpose(bank_bf(4)[:, s_ * 128:(s_ + 1) * 128],
                                   c.sqd[0:128, s_ * 128:(s_ + 1) * 128], c.eye_sb)
            return last
        p.op("tensor", f_tr, after=("f2ln" + L, "fgev" + L), then="ftr" + L, fence=True)

        def f_ev(s, j=j):
            return s.activation(
                h2T.rearrange("p (s n) -> p s n", s=8)[:, :, j * 128:(j + 1) * 128],
                bank_bf(4).rearrange("p (s n) -> p s n", s=8)[:, :, 0:128], AF.Copy)
        p.op("scalar", f_ev, after="ftr" + L, then="fev0" + L)

        def f_g(v, j=j):
            last = None
            for s_ in range(8):
                hv = h2T.rearrange("p (s n) -> p s n", s=8)[:, s_, j * 128:(j + 1) * 128]
                last = v.tensor_scalar(hv, hv, lngc[:, s_:s_ + 1], lnbc[:, s_:s_ + 1],
                                       ALU.mult, ALU.add)
            return last
        p.op("vector", f_g, after="fev0" + L, then="fgev" + L)
    p.barrier()

    NFS = 22
    for fs in range(NFS):
        fb = fs % 2
        fw = 42 if fs == 21 else 128
        p.dma("sync", w1a[fb].rearrange("p (s n) -> p s n", s=8)[:, :, 0:fw],
              d[f"ffw1_{ly}"][:, fs * 128:fs * 128 + fw].rearrange("(s p) n -> p s n", p=128),
              after=f"w1f{fb}" + L, then=f"w1l{fb}" + L)
        p.dma("sync", w1g[fb].rearrange("p (s n) -> p s n", s=8)[:, :, 0:fw],
              d[f"ffw1_{ly}"][:, 2730 + fs * 128:2730 + fs * 128 + fw].rearrange(
                  "(s p) n -> p s n", p=128),
              after=f"w1f{fb}" + L, then=f"w1l{fb}" + L)
        for tb in range(4):
            def f_wa(t, fs=fs, fb=fb, fw=fw, tb=tb):
                last = None
                for ds in range(8):
                    last = t.matmul(bank(5)[0:fw, 0:512],
                                    w1a[fb].rearrange("p (s n) -> p s n", s=8)[:, ds, 0:fw],
                                    h2T.rearrange("p (s n) -> p s n", s=8)[:, ds,
                                                                           tb * 512:(tb + 1) * 512],
                                    start=(ds == 0), stop=(ds == 7))
                for ds in range(8):
                    last = t.matmul(bank(6)[0:fw, 0:512],
                                    w1g[fb].rearrange("p (s n) -> p s n", s=8)[:, ds, 0:fw],
                                    h2T.rearrange("p (s n) -> p s n", s=8)[:, ds,
                                                                           tb * 512:(tb + 1) * 512],
                                    start=(ds == 0), stop=(ds == 7))
                return last
            p.op("tensor", f_wa, after=(f"w1l{fb}" + L, "fgev" + L, "gvd" + L),
                 then="wam" + L, fence=True)

            def f_gg(s, fs=fs, fw=fw):
                return s.activation(gg[0:fw, :], bank(6)[0:fw, 0:512], AF.Gelu,
                                    bias=b1gc[0:fw, fs:fs + 1])
            p.op("scalar", f_gg, after=("wam" + L, "gvd2" + L), then="ggv" + L)

            def f_ga(v, fs=fs, fw=fw):
                return v.scalar_tensor_tensor(gab[0:fw, :], bank(5)[0:fw, 0:512],
                                              b1ac[0:fw, fs:fs + 1], gg[0:fw, :],
                                              ALU.add, ALU.mult)
            p.op("vector", f_ga, after=("ggv" + L, "gad" + L), then=("gav" + L, "gvd" + L, "gvd2" + L))
            p.dma("sync", c.ga_d[fs * 128:fs * 128 + fw, tb * 512:(tb + 1) * 512], gab[0:fw, :],
                  after="gav" + L, then="gad" + L)
        p.op("vector", lambda v, fb=fb: v.engine_nop(), after="gvd" + L, then=f"w1f{fb}" + L)
    p.barrier()
    if lvl < 4:
        A.reset()
        return

    for tb in range(4):
        for fs in range(NFS):
            fb = fs % 2
            fw = 42 if fs == 21 else 128
            p.dma("sync", gat[fb][0:fw, :],
                  c.ga_d[fs * 128:fs * 128 + fw, tb * 512:(tb + 1) * 512],
                  after=f"gbf{fb}" + L, then=f"gbl{fb}" + L)
            p.dma("sync", w2r[fb][0:fw, :], d[f"ffw2e{ly}"][fs * 128:fs * 128 + fw, :],
                  after=f"gbf{fb}" + L, then=f"gbl{fb}" + L)
            for tt in range(4):
                for og in range(2):
                    bk = tt * 2 + og
                    def f_w2(t, fs=fs, fb=fb, fw=fw, tt=tt, og=og, bk=bk):
                        last = t.matmul(bank(bk)[:, 0:512],
                                        gat[fb][0:fw, tt * 128:(tt + 1) * 128],
                                        w2r[fb][0:fw, og * 512:(og + 1) * 512],
                                        start=(fs == 0), stop=False, skip_group_check=True)
                        if fs == NFS - 1:
                            last = t.matmul(bank(bk)[:, 0:512], ones_r,
                                            w2brow[:, og * 512:(og + 1) * 512],
                                            start=False, stop=True, skip_group_check=True)
                        return last
                    p.op("tensor", f_w2,
                         after=(f"gbl{fb}" + L,) + ((f"w2be{bk}" + L,) if fs == 0 else ()),
                         then=("w2bm%d" % bk + L, "w2f" + L) if fs == NFS - 1
                         else ("w2f" + L,),
                         fence=(fs == NFS - 1))
            p.op("vector", lambda v, fb=fb: v.engine_nop(), after="w2f" + L,
                 then=f"gbf{fb}" + L)
        for tt in range(4):
            j = tb * 4 + tt
            p.dma("sync", c.scr[:, 0:DM], c.xs_d[j * 128:(j + 1) * 128, :],
                  after="s2f" + L, then="s2l" + L)
            for og in range(2):
                bk = tt * 2 + og
                def f_fin(v, tt=tt, og=og, bk=bk):
                    return v.tensor_tensor(c.scr[:, og * 512:(og + 1) * 512],
                                           c.scr[:, og * 512:(og + 1) * 512],
                                           bank(bk)[:, 0:512], ALU.add)
                p.op("vector", f_fin, after=("w2bm%d" % bk + L, "s2l" + L),
                     then=(f"w2be{bk}" + L, "fin%d" % og + L))
            p.dma("sync", c.xs_d[j * 128:(j + 1) * 128, :], c.scr[:, 0:DM],
                  after=("fin0" + L, "fin1" + L), then=("s2w" + L, "s2f" + L))
    p.op("sync", lambda e: None, after="s2w" + L)
    p.barrier()
    A.reset()


def emit_output_from_dram2(c):
    p = c.p
    for j in range(16):
        p.dma("sync", c.scr[:, 0:DM], c.xres2_d[j * 128:(j + 1) * 128, :], after="foc3", then="flo3")
        p.op("vector", lambda v: v.tensor_copy(c.scr2[:, 0:DM], c.scr[:, 0:DM]), after="flo3", then="foc3")
        p.dma("sync", c.out[j * 128:(j + 1) * 128, :], c.scr2[:, 0:DM], after="foc3", then="fo2")
    p.op("sync", lambda e: None, after="fo2")


def emit_output_from_sbuf(c):
    p = c.p
    for j in range(16):
        p.dma("sync", c.out[j * 128:(j + 1) * 128, :], c.xres[:, j * 1024:(j + 1) * 1024],
              then="fo2")
    p.op("sync", lambda e: None, after="fo2")


def emit_output_from_dram(c):
    p = c.p
    for j in range(TOWN // 128):
        p.dma("sync", c.scr[:, 0:DM], c.xres_d[j * 128:(j + 1) * 128, :], after=("foc", "fo2"), then="flo")
        p.dma("sync", c.scr2[:, 0:DM], c.xres_d[TOWN + j * 128:TOWN + (j + 1) * 128, :],
              after="foc", then="flo")
        def f_cmb(v):
            v.tensor_scalar(c.scr[:, 0:DM], c.scr[:, 0:DM], c.flags_sb[:, 0:1], None, ALU.mult)
            v.drain()
            return v.scalar_tensor_tensor(c.scr[:, 0:DM], c.scr2[:, 0:DM], c.flags_sb[:, 1:2],
                                          c.scr[:, 0:DM], ALU.mult, ALU.add)
        p.op("vector", f_cmb, after="flo", then="foc")
        p.dma("sync", c.out[j * 128:(j + 1) * 128, :], c.scr[:, 0:DM], after="foc", then="fo2")
    p.op("sync", lambda e: None, after="fo2")



def _bc_pc(v):
    v = np.asarray(v, np.float32).ravel()
    return np.ascontiguousarray(np.broadcast_to(v.reshape(1, -1), (128, v.size)))


def _slab_cols(v):
    return np.ascontiguousarray(np.asarray(v, np.float32).reshape(8, 128).T)


def stage_inputs(I):
    tri = np.triu(np.ones((128, 128), np.float32))
    tri[127, :] = 1.0
    eye = np.eye(128, dtype=np.float32)
    maps = []
    for core in range(8):
        b, r = core // 2, core % 2
        h0 = r * HH
        m = {}
        m["x"] = np.ascontiguousarray(I["L0_attn_x"][b])
        W_in = I["L0_attn_W_in"]
        m["w_inT"] = np.ascontiguousarray(
            np.concatenate([W_in[h0:h0 + HH].T, W_in[HID + h0:HID + h0 + HH].T], axis=1))
        cw = I["L0_attn_conv_w"][h0:h0 + HH, 0, :]
        m["convw"] = np.ascontiguousarray(cw.reshape(8, 128, 3).transpose(1, 0, 2).reshape(128, 24))
        m["convb"] = _slab_cols(I["L0_attn_conv_b"][h0:h0 + HH])
        m["w_xprojT"] = np.ascontiguousarray(I["L0_attn_W_xproj"][:, h0:h0 + HH].T)
        bd = I["L0_attn_b_dt"][h0:h0 + HH].astype(np.float32)
        b0v = bd.astype(BF).astype(np.float32)
        b1v = (bd - b0v).astype(BF).astype(np.float32)
        b2v = (bd - b0v - b1v.astype(np.float32)).astype(np.float32)
        m["w_dtT_ext"] = np.ascontiguousarray(
            np.concatenate([I["L0_attn_W_dt"][h0:h0 + HH].T,
                            b0v.reshape(1, -1), b1v.reshape(1, -1), b2v.reshape(1, -1)], axis=0))
        m["alog_rep"] = _bc_pc(I["L0_attn_A_log"])
        m["d_cols"] = _slab_cols(I["L0_attn_D"][h0:h0 + HH])
        m["w_outT"] = np.ascontiguousarray(I["L0_attn_W_out"][:, h0:h0 + HH].T)
        m["tri"] = tri
        m["eye"] = eye
        m["router_wT"] = np.ascontiguousarray(I["L0_ffn_router_w"].T)
        m["router_b"] = _bc_pc(I["L0_ffn_router_b"])
        m["iota8"] = _bc_pc(np.arange(8, dtype=np.float32) - 1000.0)
        m["iota8p"] = _bc_pc(np.arange(8, dtype=np.float32))
        m["expert_wT"] = np.ascontiguousarray(I["L0_ffn_expert_w"].transpose(0, 2, 1))
        m["expert_b"] = np.ascontiguousarray(I["L0_ffn_expert_b"])
        fl = np.zeros(8, np.float32)
        fl[0] = 1.0 if r == 0 else 0.0
        fl[1] = 1.0 if r == 1 else 0.0
        m["flags"] = _bc_pc(fl)
        m["wq"] = np.ascontiguousarray(I["L1_attn_wq"]).astype(BF)
        m["wk"] = np.ascontiguousarray(I["L1_attn_wk"]).astype(BF)
        m["wv"] = np.ascontiguousarray(I["L1_attn_wv"]).astype(BF)
        m["wo"] = np.ascontiguousarray(I["L1_attn_wo"]).astype(BF)
        m["proj"] = np.ascontiguousarray(I["L1_attn_proj"]).astype(BF)
        m["la_w1"] = np.ascontiguousarray(I["L1_attn_w1"]).astype(BF)
        m["la_w2ext"] = np.ascontiguousarray(np.concatenate(
            [I["L1_attn_w2"], I["L1_attn_b2"].reshape(1, -1)], axis=0)).astype(BF)
        m["la_b1c"] = np.ascontiguousarray(
            np.asarray(I["L1_attn_b1"], np.float32).reshape(32, 128).T)
        m["g1c"] = _slab_cols(I["L1_attn_g1"])
        m["b1c"] = _slab_cols(I["L1_attn_beta1"])
        m["g2c"] = _slab_cols(I["L1_attn_g2"])
        m["b2c"] = _slab_cols(I["L1_attn_beta2"])
        m["triu_pure"] = np.triu(np.ones((128, 128), np.float32)).astype(BF)
        m["onescol"] = np.ones((128, 8), BF)
        m["onesrow"] = np.ones((1, 128), BF)
        for l in range(2):
            m[f"ffqkv{l}"] = np.ascontiguousarray(I["L1_ffn_wqkv"][l]).astype(BF)
            m[f"ffwo{l}"] = np.ascontiguousarray(I["L1_ffn_wo"][l]).astype(BF)
            m[f"ffw1_{l}"] = np.ascontiguousarray(I["L1_ffn_w1"][l]).astype(BF)
            m[f"ffw2e{l}"] = np.ascontiguousarray(np.concatenate(
                [I["L1_ffn_w2"][l], I["L1_ffn_b2"][l].reshape(1, -1)], axis=0)).astype(BF)
            m[f"ffqs{l}"] = _bc_pc(I["L1_ffn_q_scale"][l])
            m[f"ffks{l}"] = _bc_pc(I["L1_ffn_k_scale"][l])
            m[f"fflng{l}"] = _slab_cols(I["L1_ffn_ln_g"][l])
            m[f"fflnb{l}"] = _slab_cols(I["L1_ffn_ln_b"][l])
            b1f = np.asarray(I["L1_ffn_b1"][l], np.float32)
            ba = np.zeros(22 * 128, np.float32); ba[0:2730] = b1f[0:2730]
            bg = np.zeros(22 * 128, np.float32); bg[0:2730] = b1f[2730:5460]
            m[f"ffb1a{l}"] = np.ascontiguousarray(ba.reshape(22, 128).T)
            m[f"ffb1g{l}"] = np.ascontiguousarray(bg.reshape(22, 128).T)
        half_idx = np.arange(0, 64, 2, dtype=np.float64)
        inv_freq = 1.0 / 10000.0 ** (half_idx / 64.0)
        tpos = (r * TOWN + np.arange(TOWN, dtype=np.float64))
        freqs = np.concatenate([tpos[:, None] * inv_freq[None, :]] * 2, axis=1)
        base = (half_idx + 0.4 * 64.0) / (1.4 * 64.0)
        power = (tpos - T // 2) / float(512 // 2)
        scale = np.concatenate([base[None, :] ** power[:, None]] * 2, axis=1)
        cosf, sinf = np.cos(freqs), np.sin(freqs)
        m["rcq"] = (cosf * scale * 8.0).astype(np.float32)
        m["rsq"] = (sinf * scale * 8.0).astype(np.float32)
        m["rck"] = (cosf / scale).astype(np.float32)
        m["rsk"] = (sinf / scale).astype(np.float32)
        pp = np.arange(128)[:, None]
        cw = np.arange(640)[None, :]
        band = np.where((cw >= pp) & (cw <= pp + 512), 0.0, -1e30).astype(np.float32)
        lm = np.empty((5, 128, 640), np.float32)
        lm[4] = band
        for tq in range(4):
            v_ = band.copy()
            if r == 0:
                v_[:, 0:512 - tq * 128] = -1e30
            lm[tq] = v_
        m["lmask"] = lm.astype(BF)
        maps.append(m)
    return maps


def kernel(**inputs):
    if "nc" not in _NC_CACHE:
        _NC_CACHE["nc"] = build_program()
    nc = _NC_CACHE["nc"]
    maps = stage_inputs(inputs)
    res = run_bass_kernel_spmd(nc, maps, core_ids=list(range(8)))
    out = np.empty((B, T, DM), np.float32)
    for core in range(8):
        b, r = core // 2, core % 2
        out[b, r * TOWN:(r + 1) * TOWN] = res.results[core]["out"]
    if DBG:
        kernel.taps = [res.results[core] for core in range(8)]
    return out


if __name__ == "__main__":
    nc = build_program()
    print("built ok")
